# version 20 (final)
# baseline (speedup 1.0000x reference)
import contextlib
import os
import numpy as np
import ml_dtypes
import concourse.bass as bass
import concourse.mybir as mybir
from concourse.bass_utils import run_bass_kernel_spmd


F32 = mybir.dt.float32
BF16 = mybir.dt.bfloat16
AF = mybir.ActivationFunctionType
ALU = mybir.AluOpType
AX = mybir.AxisListType

P = 128
D = 1024
NDT = 8
NN = 16
TOK = 2048
ROWS = TOK + 1
EPS = 1e-5
W513 = 514

BFDT = ml_dtypes.bfloat16


class Sched:

    STREAMS = ("v", "s", "t", "g", "q")
    SEMS = ("v", "s", "t", "g", "q", "sq", "qd", "gd")
    DMA_SEM = {"q": "qd", "g": "gd", "s": "sq"}

    def __init__(self):
        self.ops = {e: [] for e in self.STREAMS}
        self.cnt = {e: 0 for e in self.SEMS}
        self.writer = {}
        self.readers = {}
        self.bar_vals = {s: {0} for s in self.SEMS}
        self.dbg_rw = []

    def emit(self, eng, fn, reads=(), writes=(), waits=(), dma=False, sem=None,
             hard_waits=()):
        if dma:
            sem = sem or self.DMA_SEM[eng]
        else:
            sem = sem or eng
        amt = 16 if dma else 1
        wl = list(waits)
        for b in reads:
            if b in self.writer:
                wl.append(self.writer[b])
        for b in writes:
            if b in self.writer:
                wl.append(self.writer[b])
            wl.extend(self.readers.get(b, ()))
        wmax = {}
        keep_same = os.environ.get("BASS_SIM_FP64") == "1"
        for (s, val) in wl:
            if s == eng and not keep_same:
                continue
            if val > wmax.get(s, -1):
                wmax[s] = val
        for (s, val) in hard_waits:
            if val > wmax.get(s, -1):
                wmax[s] = val
        self.cnt[sem] += amt
        tok = (sem, self.cnt[sem])
        self.ops[eng].append((fn, sorted(wmax.items()), sem, amt, (tuple(reads), tuple(writes))))
        self.dbg_rw.append((eng, sorted(wmax.items()), tuple(reads), tuple(writes)))
        for b in writes:
            self.writer[b] = tok
            self.readers[b] = []
        for b in reads:
            self.readers.setdefault(b, []).append(tok)
        return tok

    def tok(self, sem):
        return (sem, self.cnt[sem])

    DMA_SEMS = ("qd", "gd", "sq")

    def dma_barrier(self, eng, rebind=(), scrub=()):
        dsem = self.DMA_SEM[eng]
        tgt = self.cnt[dsem]
        self.bar_vals[dsem].add(tgt)
        tok = self.emit(eng, lambda e: e.nop(), waits=[(dsem, tgt)])
        for b in rebind:
            if b in self.writer:
                self.writer[b] = tok
        for b in scrub:
            if b in self.readers:
                self.readers[b] = [t for t in self.readers[b]
                                   if t[0] not in self.DMA_SEMS] + [tok]
        return tok

    def check_dma_waits(self):
        import sys
        bad = 0
        for eng, waits, rd, wr in self.dbg_rw:
            for s, val in waits:
                if s in self.DMA_SEMS and val not in self.bar_vals[s]:
                    print(f"BAD DMA WAIT eng={eng} {s}>={val} reads={rd} writes={wr}",
                          file=sys.stderr)
                    bad += 1
                    if bad > 10:
                        return
    def replay(self, eng, eng_obj, sems):
        last = {}
        for idx, (fn, waits, sem, amt, dbg) in enumerate(self.ops[eng]):
            for s, val in waits:
                if last.get(s, -1) >= val:
                    continue
                eng_obj.wait_ge(sems[s], val)
                last[s] = val
            try:
                inst = fn(eng_obj)
            except Exception:
                print(f"REPLAY FAIL eng={eng} idx={idx} rw={dbg}")
                raise
            inst.then_inc(sems[sem], amt)


def build_program():
    nc = bass.Bass("TRN2", target_bir_lowering=False)

    xs = nc.dram_tensor("xs", [ROWS, D], F32, kind="ExternalInput").ap()
    ys = nc.dram_tensor("ys", [TOK, D], F32, kind="ExternalOutput").ap()
    win = {}
    for name, shape, dt in [
        ("Wl1", [D, P], F32), ("Wl2", [P, D], F32), ("Wf1", [2 * D, P], F32),
        ("Wf2", [P, D], F32), ("W1", [D, D], F32), ("W23", [D, 32], F32),
        ("WrT", [D, D], F32), ("WkT", [D, D], F32), ("WvT", [D, D], F32),
        ("A", [D, NN], F32), ("S", [D, NN], F32),
        ("bl1", [P, 1], F32), ("bl2", [D, 1], F32), ("bf1", [P, 1], F32),
        ("bf2", [D, 1], F32), ("b1", [D, 1], F32), ("b23", [32, 1], F32),
        ("mur", [D, 1], F32), ("muk", [D, 1], F32),
        ("identf", [P, P], F32), ("identb", [P, P], BF16), ("maskv", [P, 1], F32),
    ]:
        win[name] = nc.dram_tensor(name, shape, dt, kind="ExternalInput").ap()
    qscr = nc.dram_tensor("qscr", [NN, W513], BF16, kind="ExternalOutput").ap()
    cbscr = nc.dram_tensor("cbscr", [1, W513], F32, kind="ExternalOutput").ap()
    diagscr = nc.dram_tensor("diagscr", [P, NDT * NN * P], BF16, kind="ExternalOutput").ap()

    es = contextlib.ExitStack()
    sb = lambda n, s, d=F32: es.enter_context(nc.sbuf_tensor(n, s, d))
    psb = lambda n, s, d=F32: es.enter_context(nc.psum_tensor(n, s, d))

    with es:
        w1b = sb("w1b", [P, 64 * P], BF16)
        wrtb = sb("wrtb", [P, 64 * P], BF16)
        wktb = sb("wktb", [P, 64 * P], BF16)
        wvtb = sb("wvtb", [P, 64 * P], BF16)
        wf1b = sb("wf1b", [P, 16 * P], BF16)
        wl1b = sb("wl1b", [P, 8 * P], BF16)
        wl2b = sb("wl2b", [P, 8 * P], BF16)
        wf2b = sb("wf2b", [P, 8 * P], BF16)
        w23b = sb("w23b", [P, 8 * 32], BF16)
        A_sb = sb("A_sb", [P, NDT * NN])
        S_sb = sb("S_sb", [P, NDT * NN])
        identb = sb("identb_s", [P, P], BF16)
        identf = sb("identf_s", [P, P])
        onesf = sb("onesf", [16, 1])
        bl1 = sb("bl1_s", [P, 1])
        bl2 = sb("bl2_s", [P, NDT])
        bf1 = sb("bf1_s", [P, 1])
        bf2 = sb("bf2_s", [P, NDT])
        b1s = sb("b1_s", [P, NDT])
        b23 = sb("b23_s", [32, 1])
        murs = sb("mur_s", [P, NDT])
        muks = sb("muk_s", [P, NDT])
        maskp = sb("mask_s", [P, 1])
        ceps = sb("ceps", [P, 1])
        cone = sb("cone", [P, 1])
        carrycol = sb("carrycol", [P, NDT], BF16)

        xstage = sb("xstage", [P, 4 * D])
        xhalo = sb("xhalo", [1, D])
        accF = sb("accF", [P, 8 * W513])
        dff = sb("dff", [P, 8 * W513])
        qxx = sb("qxx", [P, NN * W513], BF16)
        bf1a = sb("bf1a", [P, 8 * W513], BF16)
        bf2a = sb("bf2a", [P, 8 * W513], BF16)
        e2k = sb("e2k", [P, 8 * W513], BF16)
        gbuf = sb("gbuf", [P, 2 * W513])
        ypa = sb("ypa", [P, W513])
        wpa = sb("wpa", [P, 2 * W513])
        vza = sb("vza", [P, 4 * W513], BF16)
        cma = sb("cma", [P, 2 * W513])
        cmbf = sb("cmbf", [16, W513], BF16)
        diagbuf = sb("diagbuf", [P, 2 * NN * P], BF16)
        stats = sb("stats", [P, 24])
        hstats = sb("hstats", [1, 8])

        lnstage = wpa[:, 0:512].bitcast(BF16)
        lnhalo = wpa[0:1, 513:1025].bitcast(BF16)

        ps_mlp = [psb("ps_mlp0", [P, 512]), psb("ps_mlp1", [P, 512])]
        ps_acc = psb("ps_acc", [P, 1024])
        ps_tr = psb("ps_tr", [P, 1024])
        ps_bf = psb("ps_bf", [P, 1024], BF16)
        ps_sm = psb("ps_sm", [P, 512])

        nc.const_aps.register(F32, EPS, ceps[:])
        nc.const_aps.register(F32, 1.0, cone[:])

        sems = {}
        with contextlib.ExitStack() as ses:
            for s in Sched.SEMS:
                sems[s] = ses.enter_context(nc.semaphore(f"sem_{s}"))
            block = ses.enter_context(nc.Block())
            sc = Sched()

            sc.emit("v", lambda v: v.memset(ceps[:], EPS), writes=["ceps"])
            sc.emit("v", lambda v: v.memset(cone[:], 1.0), writes=["cone"])
            sc.emit("v", lambda v: v.memset(onesf[:], 1.0), writes=["onesf"])
            sc.emit("v", lambda v: v.memset(qxx[:], 0.0),
                    writes=[f"q{n}" for n in range(NN)]
                           + [f"xr{i}" for i in range(NDT)]
                           + [f"xk{i}" for i in range(NDT)]
                           + [f"y2s{l}" for l in range(4)]
                           + [f"slot{k}" for k in range(8, 12)])
            sc.emit("v", lambda v: v.memset(wpa[:, 512:514], 0.0),
                    writes=["wp0"])

            def stage_cast(nm_src_ap, dst_ap, slot_name, slot_ap):
                t_l = sc.emit("q", lambda q, o=slot_ap, i=nm_src_ap:
                              q.dma_start(out=o, in_=i),
                              writes=[slot_name], dma=True)
                return sc.emit("v", lambda v, s=slot_ap, d=dst_ap:
                               v.tensor_copy(out=d, in_=s),
                               reads=[slot_name], waits=[t_l])

            for lt in range(4):
                sc.emit("q", lambda q, l=lt:
                        q.dma_start(out=xstage[:, l * D:(l + 1) * D],
                                    in_=xs[1 + l * P: 1 + (l + 1) * P, :]),
                        writes=[f"xst{lt}"], dma=True)
            sc.emit("q", lambda q: q.dma_start(out=xhalo[:], in_=xs[0:1, :]),
                    writes=["xhalo"], dma=True)
            sc.dma_barrier("q", rebind=[f"xst{l}" for l in range(4)] + ["xhalo"])

            for nm, dst, rearr in [
                ("A", A_sb, True), ("S", S_sb, True), ("bl2", bl2, True),
                ("bf2", bf2, True), ("b1", b1s, True), ("mur", murs, True),
                ("muk", muks, True),
                ("identb", identb, False), ("identf", identf, False),
                ("bl1", bl1, False), ("bf1", bf1, False), ("b23", b23, False),
                ("maskv", maskp, False),
            ]:
                src2 = (win[nm].rearrange("(a p) c -> p a c", p=P) if rearr
                        else win[nm])
                sc.emit("q", lambda q, o=dst, i=src2:
                        q.dma_start(out=o[:], in_=i,
                                    allow_slow_non_contiguous=True),
                        writes=[nm + "_r"], dma=True)
            sc.dma_barrier("q", rebind=[nm_ + "_r" for nm_ in (
                "A", "S", "bl2", "bf2", "b1", "mur", "muk", "identb", "identf",
                "bl1", "bf1", "b23", "maskv")])

            jobs = []
            for nm, dstb, is_tiled in [("Wl1", wl1b, True), ("Wl2", wl2b, False),
                                       ("Wf2", wf2b, False)]:
                s_ = (win[nm].rearrange("(a p) c -> p a c", p=P) if is_tiled
                      else win[nm])
                jobs.append((s_, dstb[:, 0:1024], 1024, "w_" + nm))
            for half in range(2):
                jobs.append((win["Wf1"][half * 1024:(half + 1) * 1024, :]
                             .rearrange("(a p) c -> p a c", p=P),
                             wf1b[:, half * 1024:(half + 1) * 1024], 1024, "w_Wf1"))
            jobs.append((win["W23"].rearrange("(a p) c -> p a c", p=P),
                         w23b[:, 0:256], 256, "w_W23"))
            big = [("W1", w1b), ("WrT", wrtb), ("WkT", wktb), ("WvT", wvtb)]
            for nm, dstb in big:
                for dt in range(NDT):
                    jobs.append((win[nm][dt * P:(dt + 1) * P, :],
                                 dstb[:, dt * 1024:(dt + 1) * 1024], 1024, "w_" + nm))

            qxxf = qxx.bitcast(F32)
            bf2f = bf2a.bitcast(F32)
            e2kf = e2k.bitcast(F32)
            vzaf = vza.bitcast(F32)
            dgbf = diagbuf.bitcast(F32)
            slots = ([("accF", accF[:, i * 1024:(i + 1) * 1024]) for i in range(4)]
                     + [("dff", dff[:, i * 1024:(i + 1) * 1024]) for i in range(4)]
                     + [("qxx", qxxf[:, i * 1024:(i + 1) * 1024]) for i in range(4)]
                     + [("bf2a", bf2f[:, i * 1024:(i + 1) * 1024]) for i in range(2)]
                     + [("e2k", e2kf[:, i * 1024:(i + 1) * 1024]) for i in range(2)]
                     + [("dgb", dgbf[:, i * 1024:(i + 1) * 1024]) for i in range(2)]
                     + [("vza", vzaf[:, 0:1024]), ("cma", cma[:, 0:1024])])
            SLOT_ARENAS = sorted({sn for sn, _ in slots})

            def emit_round(grp, off):
                bars = []
                for k, (s_, d_, w_, wn_) in enumerate(grp):
                    qq = "q" if k % 2 == 0 else "g"
                    sn, sap = slots[off + k]
                    sc.emit(qq, lambda q, o=sap, i=s_, w=w_:
                            q.dma_start(out=o[:, 0:w] if w != 1024 else o, in_=i),
                            writes=[f"slot{off + k}"], dma=True)
                sl = [f"slot{off + k}" for k in range(len(grp))]
                bars.append(sc.dma_barrier("q", rebind=sl))
                bars.append(sc.dma_barrier("g", rebind=sl))
                toks = []
                for k, (s_, d_, w_, wn_) in enumerate(grp):
                    sn, sap = slots[off + k]
                    toks.append(sc.emit("v", lambda v, s2=sap, d=d_, w=w_:
                                        v.tensor_copy(out=d, in_=s2[:, 0:w] if w != 1024 else s2),
                                        reads=[f"slot{off + k}"], writes=[wn_],
                                        waits=bars))
                return toks

            r1 = jobs[0:20]
            r2 = jobs[20:]
            for k, (s_, d_, w_, wn_) in enumerate(r1):
                qq = "q" if k % 2 == 0 else "g"
                sn, sap = slots[k]
                sc.emit(qq, lambda q, o=sap, i=s_, w=w_:
                        q.dma_start(out=o[:, 0:w] if w != 1024 else o, in_=i),
                        writes=[f"slot{k}"], dma=True)
            _sl1 = [f"slot{k}" for k in range(len(r1))]
            r1bars = [sc.dma_barrier("q", rebind=_sl1),
                      sc.dma_barrier("g", rebind=_sl1)]

            def emit_deferred_init():
                toks = []
                for k, (s_, d_, w_, wn_) in enumerate(r1):
                    sn, sap = slots[k]
                    toks.append(sc.emit("v", lambda v, s2=sap, d=d_, w=w_:
                                        v.tensor_copy(out=d, in_=s2[:, 0:w] if w != 1024 else s2),
                                        reads=[f"slot{k}"], writes=[wn_],
                                        waits=r1bars))
                toks.extend(emit_round(r2, 0))
                for tok in toks:
                    for an in SLOT_ARENAS:
                        sc.readers.setdefault(an, []).append(tok)
                for tok in toks:
                    for i in range(NDT):
                        sc.readers.setdefault(f"xr{i}", []).append(tok)
                        sc.readers.setdefault(f"xk{i}", []).append(tok)
                    for an in ("vzaA", "vzaB", "vzaC", "vzaD", "ypa",
                               "cma", "cmash", "diagb0", "diagb1"):
                        sc.readers.setdefault(an, []).append(tok)
                    for n in range(NN):
                        sc.readers.setdefault(f"q{n}", []).append(tok)
                for dt_ in range(NDT):
                    pi = dt_ % 2
                    for n_ in range(NN):
                        k = dt_ * NN + n_
                        sc.emit("v", lambda v, k=k, p=pi, n=n_:
                                v.tensor_scalar(out=diagbuf[:, (p * NN + n) * P:(p * NN + n + 1) * P],
                                                in0=identf[:],
                                                scalar1=S_sb[:, k:k + 1], scalar2=None,
                                                op0=ALU.mult),
                                reads=["identf_r", "S_r"],
                                writes=[f"diagb{pi}"] if n_ == 0 else [])
                    sc.writer[f"diagb{pi}"] = sc.tok("v")
                    sc.emit("g", lambda g, d=dt_, p=pi:
                            g.dma_start(out=diagscr[:, d * NN * P:(d + 1) * NN * P],
                                        in_=diagbuf[:, p * NN * P:(p + 1) * NN * P]),
                            reads=[f"diagb{pi}"], writes=["diagscr_d"], dma=True)
                    sc.dma_barrier("g", rebind=["diagscr_d"],
                                   scrub=[f"diagb{pi}"])
                return toks

            def emit_ln(src_ap, rows, dst_ap, extra_waits, scr_ap, st, st_name,
                        src_bufs, scr_bufs, dst_bufs):
                r = rows
                t1 = sc.emit("v", lambda v: v.reduce_sum(out=st[0:r, 0:1], in_=src_ap,
                                                         axis=AX.X),
                             reads=src_bufs, writes=[st_name], waits=extra_waits)
                t2a = sc.emit("v", lambda v: v.scalar_tensor_tensor(out=scr_ap, in0=src_ap,
                                                              scalar=1.0, in1=src_ap,
                                                              op0=ALU.mult, op1=ALU.mult,
                                                              accum_out=st[0:r, 1:2]),
                        reads=src_bufs, writes=scr_bufs + [st_name])
                sc.emit("v", lambda v: v.tensor_scalar(out=st[0:r, 2:3], in0=st[0:r, 0:1],
                                                       scalar1=1.0 / D, scalar2=None,
                                                       op0=ALU.mult),
                        reads=[st_name], writes=[st_name], hard_waits=[t1])
                sc.emit("v", lambda v: v.tensor_scalar(out=st[0:r, 3:4], in0=st[0:r, 1:2],
                                                       scalar1=1.0 / D, scalar2=None,
                                                       op0=ALU.mult),
                        reads=[st_name], writes=[st_name], hard_waits=[t2a])
                sc.emit("v", lambda v: v.scalar_tensor_tensor(out=st[0:r, 3:4],
                                                              in0=st[0:r, 2:3],
                                                              scalar=st[0:r, 2:3],
                                                              in1=st[0:r, 3:4],
                                                              op0=ALU.mult,
                                                              op1=ALU.subtract),
                        reads=[st_name], writes=[st_name])
                sc.emit("s", lambda s_: s_.activation(out=st[0:r, 4:5], in_=st[0:r, 3:4],
                                                      func=AF.Ln, bias=EPS, scale=-1.0),
                        reads=[st_name, "ceps"], writes=[st_name])
                sc.emit("s", lambda s_: s_.activation(out=st[0:r, 4:5], in_=st[0:r, 4:5],
                                                      func=AF.Exp, scale=-0.5),
                        reads=[st_name], writes=[st_name])
                return sc.emit("v", lambda v: v.tensor_scalar(out=dst_ap, in0=src_ap,
                                                              scalar1=st[0:r, 2:3],
                                                              scalar2=st[0:r, 4:5],
                                                              op0=ALU.subtract,
                                                              op1=ALU.mult),
                               reads=src_bufs + [st_name], writes=dst_bufs)

            def emit_stats_group(srcs, src_bufs_list, scr_ap):
                nt = len(srcs)
                for i, (src_ap, rb) in enumerate(zip(srcs, src_bufs_list)):
                    t_red = sc.emit("v", lambda v, s=src_ap, i=i:
                            v.reduce_sum(out=stats[:, i:i + 1], in_=s, axis=AX.X),
                            reads=rb, writes=["stats"])
                    t_acc = sc.emit("v", lambda v, s=src_ap, i=i, scr=scr_ap:
                            v.scalar_tensor_tensor(out=scr, in0=s, scalar=1.0, in1=s,
                                                   op0=ALU.mult, op1=ALU.mult,
                                                   accum_out=stats[:, 4 + i:5 + i]),
                            reads=rb, writes=["gbuf0", "gbuf1", "stats"])
                    sc.emit("v", lambda v, i=i:
                            v.tensor_scalar(out=stats[:, 8 + i:9 + i],
                                            in0=stats[:, i:i + 1],
                                            scalar1=1.0 / D, scalar2=None,
                                            op0=ALU.mult),
                            reads=["stats"], writes=["stats"], hard_waits=[t_red])
                    sc.emit("v", lambda v, i=i:
                            v.tensor_scalar(out=stats[:, 12 + i:13 + i],
                                            in0=stats[:, 4 + i:5 + i],
                                            scalar1=1.0 / D, scalar2=None,
                                            op0=ALU.mult),
                            reads=["stats"], writes=["stats"], hard_waits=[t_acc])
                    sc.emit("v", lambda v, i=i:
                            v.scalar_tensor_tensor(out=stats[:, 12 + i:13 + i],
                                                   in0=stats[:, 8 + i:9 + i],
                                                   scalar=stats[:, 8 + i:9 + i],
                                                   in1=stats[:, 12 + i:13 + i],
                                                   op0=ALU.mult, op1=ALU.subtract),
                            reads=["stats"], writes=["stats"])
                sc.emit("s", lambda s_, n=nt:
                        s_.activation(out=stats[:, 12:12 + n], in_=stats[:, 12:12 + n],
                                      func=AF.Ln, bias=EPS, scale=-1.0),
                        reads=["stats", "ceps"], writes=["stats"])
                sc.emit("s", lambda s_, n=nt:
                        s_.activation(out=stats[:, 12:12 + n], in_=stats[:, 12:12 + n],
                                      func=AF.Exp, scale=-0.5),
                        reads=["stats"], writes=["stats"])
                sc.emit("v", lambda v, n=nt:
                        v.scalar_tensor_tensor(out=stats[:, 16:16 + n],
                                               in0=stats[:, 8:8 + n], scalar=-1.0,
                                               in1=stats[:, 12:12 + n],
                                               op0=ALU.mult, op1=ALU.mult),
                        reads=["stats"], writes=["stats"])

            def emit_ln_apply(src_ap, i, dst_ap, src_bufs, dst_bufs):
                return sc.emit("s", lambda s_, s=src_ap, d=dst_ap, i=i:
                               s_.activation(out=d, in_=s, func=AF.Identity,
                                             scale=stats[:, 12 + i:13 + i],
                                             bias=stats[:, 16 + i:17 + i]),
                               reads=src_bufs + ["stats"], writes=dst_bufs)

            F = W513

            for ci in range(4):
                c0 = ci == 0
                FW = 513 if c0 else 512
                acols = [(0, 512), (512, 513)] if c0 else [(0, 512)]
                row0 = ci * 512 + 1
                LO = 1 if c0 else 0

                TKO = 1 if c0 else 0

                if not c0:
                    for lt in range(4):
                        qq = "q" if lt % 2 == 0 else "s"
                        sc.emit(qq, lambda q, l=lt, r=row0:
                                q.dma_start(out=xstage[:, l * D:(l + 1) * D],
                                            in_=xs[r + l * P: r + (l + 1) * P, :]),
                                writes=[f"xst{lt}"], dma=True)
                        sc.dma_barrier(qq, rebind=[f"xst{lt}"])

                ln1_parts = []
                emit_stats_group([xstage[:, lt * D:(lt + 1) * D] for lt in range(4)],
                                 [[f"xst{lt}"] for lt in range(4)], gbuf[:, 0:1024])
                for lt in range(4):
                    src = xstage[:, lt * D:(lt + 1) * D]
                    emit_ln_apply(src, lt, lnstage[:, 0:1024], [f"xst{lt}"],
                                  ["lnstage", "wp0"])
                    for dt in range(NDT):
                        sc.emit("t", lambda t, d=dt:
                                t.transpose(out=ps_bf[:, d * P:(d + 1) * P],
                                            in_=lnstage[:, d * P:(d + 1) * P],
                                            identity=identb[:]),
                                reads=["lnstage", "wp0", "identb_r"], writes=["ps_bf"])
                    col = TKO + lt * P
                    ln1_parts.append(
                        sc.emit("v", lambda v, c=col:
                                [v.tensor_copy(out=bf1a[:, d * F + c: d * F + c + P],
                                               in_=ps_bf[:, d * P:(d + 1) * P])
                                 for d in range(NDT)][-1],
                                reads=["ps_bf"], writes=["bf1a"]))
                if c0:
                    tlnh = emit_ln(xhalo[:], 1, lnhalo[0:1, 0:1024], [],
                                   gbuf[0:1, 0:1024], hstats, "hstats",
                                   ["xhalo"], ["gbuf0", "gbuf1"], ["lnhalo", "wp1"])
                    for dt in range(NDT):
                        sc.emit("t", lambda t, d=dt:
                                t.transpose(out=ps_bf[0:P, 960 + 2 * d:961 + 2 * d],
                                            in_=lnhalo[0:1, d * P:(d + 1) * P],
                                            identity=identb[0:1, 0:1]),
                                reads=["lnhalo", "wp1", "identb_r"], writes=["ps_bf"])
                    ln1_parts.append(
                        sc.emit("v", lambda v:
                                [v.tensor_copy(out=bf1a[:, d * F: d * F + 1],
                                               in_=ps_bf[:, 960 + 2 * d:961 + 2 * d])
                                 for d in range(NDT)][-1],
                                reads=["ps_bf"], writes=["bf1a"]))

                if c0:
                    emit_deferred_init()
                pp = [0]

                def mmacc(stat_fn, mov_fn, n_acc, psname, ps_ap, extra_reads):
                    toks = []
                    for i in range(n_acc):
                        toks.append(sc.emit(
                            "t", lambda t, s=stat_fn(i), m=mov_fn(i), i=i, n=n_acc,
                            o=ps_ap: t.matmul(out=o, lhsT=s, rhs=m, start=(i == 0),
                                              stop=(i == n - 1)),
                            reads=extra_reads if i == 0 else [],
                            writes=[psname] if i == 0 else []))
                    sc.writer[psname] = toks[-1]
                    return toks[-1]

                h1 = vza[:, 0:W513]
                for (a, b) in acols:
                    pn = f"ps_mlp{pp[0]}"
                    mmacc(lambda i: wl1b[:, i * P:(i + 1) * P],
                          lambda i, a=a, b=b: bf1a[:, i * F + a: i * F + b],
                          NDT, pn, ps_mlp[pp[0]][:, 0:b - a], ["bf1a", "w_Wl1"])
                    sc.emit("s", lambda s_, a=a, b=b, p=pp[0]:
                            s_.activation(out=h1[:, a:b], in_=ps_mlp[p][:, 0:b - a],
                                          func=AF.Relu, bias=bl1[:, 0:1]),
                            reads=[pn, "bl1_r"], writes=["vzaA"])
                    pp[0] ^= 1
                for do in range(NDT):
                    for (a, b) in acols:
                        pn = f"ps_mlp{pp[0]}"
                        sc.emit("t", lambda t, d=do, a=a, b=b, p=pp[0]:
                                t.matmul(out=ps_mlp[p][:, 0:b - a],
                                         lhsT=wl2b[:, d * P:(d + 1) * P],
                                         rhs=h1[:, a:b], start=True, stop=True),
                                reads=["vzaA", "w_Wl2"], writes=[pn])
                        sc.emit("s", lambda s_, d=do, a=a, b=b, p=pp[0]:
                                s_.activation(out=bf2a[:, d * F + a: d * F + b],
                                              in_=ps_mlp[p][:, 0:b - a],
                                              func=AF.Tanh, bias=bl2[:, d:d + 1]),
                                reads=[pn, "bl2_r"], writes=["bf2a"])
                        pp[0] ^= 1
                f1 = vza[:, 2 * W513:3 * W513]
                for (a, b) in acols:
                    def mov16(i, a=a, b=b):
                        if i < 8:
                            return bf1a[:, i * F + a: i * F + b]
                        return bf2a[:, (i - 8) * F + a: (i - 8) * F + b]
                    pn = f"ps_mlp{pp[0]}"
                    mmacc(lambda i: wf1b[:, i * P:(i + 1) * P], mov16, 16,
                          pn, ps_mlp[pp[0]][:, 0:b - a], ["bf1a", "bf2a", "w_Wf1"])
                    sc.emit("s", lambda s_, a=a, b=b, p=pp[0]:
                            s_.activation(out=f1[:, a:b], in_=ps_mlp[p][:, 0:b - a],
                                          func=AF.Relu, bias=bf1[:, 0:1]),
                            reads=[pn, "bf1_r"], writes=["vzaC"])
                    pp[0] ^= 1
                for do in range(NDT):
                    for (a, b) in acols:
                        pn = f"ps_mlp{pp[0]}"
                        sc.emit("t", lambda t, d=do, a=a, b=b, p=pp[0]:
                                t.matmul(out=ps_mlp[p][:, 0:b - a],
                                         lhsT=wf2b[:, d * P:(d + 1) * P],
                                         rhs=f1[:, a:b], start=True, stop=True),
                                reads=["vzaC", "w_Wf2"], writes=[pn])
                        sc.emit("s", lambda s_, d=do, a=a, b=b, p=pp[0]:
                                s_.activation(out=accF[:, d * F + a: d * F + b],
                                              in_=ps_mlp[p][:, 0:b - a],
                                              func=AF.Identity, bias=bf2[:, d:d + 1]),
                                reads=[pn, "bf2_r"], writes=["accF"])
                        pp[0] ^= 1
                for do in range(NDT):
                    sc.emit("v", lambda v, d=do, fw=FW:
                            v.tensor_copy(out=bf1a[:, d * F: d * F + fw],
                                          in_=accF[:, d * F: d * F + fw]),
                            reads=["accF"], writes=["bf1a"])
                for ri, (a, b) in enumerate(acols):
                    po = 0 if ri == 0 else 256
                    mmacc(lambda i: w23b[:, i * 32:(i + 1) * 32],
                          lambda i, a=a, b=b: bf1a[:, i * F + a: i * F + b],
                          NDT, "ps_sm", ps_sm[0:32, po:po + b - a], ["bf1a", "w_W23"])
                    sc.emit("s", lambda s_, a=a, b=b, po=po:
                            s_.activation(out=cma[0:32, a:b], in_=ps_sm[0:32, po:po + b - a],
                                          func=AF.Identity, bias=b23[:, 0:1]),
                            reads=["ps_sm", "b23_r"], writes=["cma"])
                sc.emit("g", lambda g, fw=FW: g.dma_start(out=cma[0:16, W513:W513 + fw],
                                                   in_=cma[16:32, 0:fw]),
                        reads=["cma"], writes=["cmash"], dma=True)
                sc.dma_barrier("g", rebind=["cmash"], scrub=["cma"])
                sc.emit("v", lambda v, fw=FW: v.tensor_copy(out=cmbf[0:16, 0:fw],
                                                     in_=cma[0:16, W513:W513 + fw]),
                        reads=["cmash"], writes=["cmbf"])
                t_qw = sc.emit("g", lambda g, fw=FW: g.dma_start(out=qscr[:, 0:fw],
                                                          in_=cmbf[0:16, 0:fw]),
                               reads=["cmbf"], writes=["qscr_d"], dma=True)
                sc.emit("v", lambda v, fw=FW: v.tensor_tensor(out=cma[0:16, W513:W513 + fw],
                                                       in0=cma[0:16, 0:fw],
                                                       in1=cma[0:16, W513:W513 + fw],
                                                       op=ALU.mult),
                        reads=["cma", "cmash", "cmbf"], writes=["cmash"])
                for ri, (a, b) in enumerate(acols):
                    po = 300 if ri == 0 else 290
                    sc.emit("t", lambda t, a=a, b=b, po=po:
                            t.matmul(out=ps_sm[0:1, 0:b - a],
                                     lhsT=onesf[:], rhs=cma[0:16, W513 + a:W513 + b],
                                     start=True, stop=True),
                            reads=["cmash", "onesf"], writes=["ps_sm"])
                    sc.emit("v", lambda v, a=a, b=b:
                            v.tensor_copy(out=cma[0:1, W513 + a:W513 + b],
                                          in_=ps_sm[0:1, 0:b - a]),
                            reads=["ps_sm", "cmash"], writes=["cmash"])
                t_cbw = sc.emit("g", lambda g, fw=FW: g.dma_start(out=cbscr[:, 0:fw],
                                                           in_=cma[0:1, W513:W513 + fw]),
                               reads=["cmash"], writes=["cbscr_d"], dma=True)
                sc.dma_barrier("g", rebind=["qscr_d", "cbscr_d"],
                               scrub=["cmbf", "cmash"])
                sc.emit("g", lambda g, fw=FW:
                        g.dma_start(out=ypa[:, 0:fw],
                                    in_=cbscr[0:1, 0:fw].squeeze().partition_broadcast(P)),
                        reads=["cbscr_d"], writes=["ypa"], dma=True)
                sc.dma_barrier("g", rebind=["ypa"], scrub=["cbscr_d"])
                for n in range(NN):
                    extra = ([f"xr{i}" for i in range(NDT)]
                             + [f"xk{i}" for i in range(NDT)]) if n == 0 else []
                    sc.emit("g", lambda g, n=n, fw=FW:
                            g.dma_start(out=qxx[:, n * F: n * F + fw],
                                         in_=qscr[n:n + 1, 0:fw].squeeze()
                                         .partition_broadcast(P)),
                            reads=["qscr_d"], writes=[f"q{n}"] + extra, dma=True)
                sc.dma_barrier("g", rebind=[f"q{n}" for n in range(NN)]
                                       + [f"xr{i}" for i in range(NDT)]
                                       + [f"xk{i}" for i in range(NDT)],
                               scrub=["qscr_d"])
                for do in range(NDT):
                    for (a, b) in acols:
                        pn = f"ps_mlp{pp[0]}"
                        mmacc(lambda i, d=do: w1b[:, (i * 8 + d) * P:(i * 8 + d + 1) * P],
                              lambda i, a=a, b=b: bf1a[:, i * F + a: i * F + b],
                              NDT, pn, ps_mlp[pp[0]][:, 0:b - a], ["bf1a", "w_W1"])
                        sc.emit("s", lambda s_, d=do, a=a, b=b, p=pp[0]:
                                s_.activation(out=dff[:, d * F + a: d * F + b],
                                              in_=ps_mlp[p][:, 0:b - a],
                                              func=AF.Exp, bias=b1s[:, d:d + 1]),
                                reads=[pn, "b1_r"], writes=["dff"])
                        sc.emit("s", lambda s_, d=do, a=a, b=b:
                                s_.activation(out=dff[:, d * F + a: d * F + b],
                                              in_=dff[:, d * F + a: d * F + b],
                                              func=AF.Ln, bias=1.0),
                                reads=["dff", "cone"], writes=["dff"])
                        pp[0] ^= 1
                for do in range(NDT):
                    sc.emit("v", lambda v, d=do, fw=FW:
                            v.tensor_tensor(out=gbuf[:, 0:fw],
                                            in0=accF[:, d * F: d * F + fw],
                                            in1=dff[:, d * F: d * F + fw], op=ALU.mult),
                            reads=["accF", "dff"], writes=["gbuf0", "gbuf1"])
                    sc.emit("v", lambda v, d=do, fw=FW:
                            v.tensor_tensor(out=e2k[:, d * F: d * F + fw],
                                            in0=gbuf[:, 0:fw], in1=ypa[:, 0:fw],
                                            op=ALU.mult),
                            reads=["gbuf0", "gbuf1", "ypa"], writes=["e2k"])

                def load_diag(dt):
                    pi = dt % 2
                    sc.emit("g", lambda g, d=dt, p=pi:
                            g.dma_start(out=diagbuf[:, p * NN * P:(p + 1) * NN * P],
                                        in_=diagscr[:, d * NN * P:(d + 1) * NN * P]),
                            reads=["diagscr_d"], writes=[f"diagb{pi}"], dma=True)
                    sc.dma_barrier("g", rebind=[f"diagb{pi}"], scrub=["diagscr_d"])

                load_diag(0)
                load_diag(1)
                evac_prev = [None]
                for dt in range(NDT):
                    pacc = ps_acc if dt % 2 == 0 else ps_tr
                    pan = "ps_acc" if dt % 2 == 0 else "ps_tr"
                    for ri, (a, b) in enumerate(acols):
                        pcol = a if ri == 0 else 512
                        sc.emit("t", lambda t, d=dt, a=a, b=b, pc=pcol, pa=pacc:
                                t.matmul(out=pa[:, pc:pc + b - a], lhsT=identb[:],
                                         rhs=e2k[:, d * F + a: d * F + b],
                                         start=True, stop=False),
                                reads=["e2k", "identb_r"],
                                writes=[pan] if ri == 0 else [])
                    def emit_g(n, dt=dt, FW=FW):
                        pi = n % 2
                        gsl = gbuf[:, pi * W513: pi * W513 + FW]
                        ac = dt * NN + n
                        sc.emit("s", lambda s_, d=dt, a=ac, g=gsl, fw=FW:
                                s_.activation(out=g, in_=dff[:, d * F: d * F + fw],
                                              func=AF.Exp, scale=A_sb[:, a:a + 1]),
                                reads=["dff", "A_r"], writes=[f"gbuf{pi}"])

                    def emit_ypwp(n, dt=dt, FW=FW):
                        pi = n % 2
                        p2 = (n // 2) % 2
                        gsl = gbuf[:, pi * W513: pi * W513 + FW]
                        wb = wpa if p2 == 0 else cma
                        wsl = wb[:, pi * W513: pi * W513 + FW]
                        wnm = [f"wp{pi}"] if p2 == 0 else ["cma", "cmash"]
                        ac = dt * NN + n
                        sc.emit("v", lambda v, g=gsl, fw=FW:
                                v.tensor_scalar(out=ypa[:, 0:fw], in0=g, scalar1=1.0,
                                                scalar2=0.0, op0=ALU.subtract,
                                                op1=ALU.min),
                                reads=[f"gbuf{pi}"], writes=["ypa"])
                        sc.emit("v", lambda v, d=dt, a=ac, w=wsl, fw=FW:
                                v.scalar_tensor_tensor(out=w, in0=dff[:, d * F: d * F + fw],
                                                       scalar=A_sb[:, a:a + 1],
                                                       in1=ypa[:, 0:fw],
                                                       op0=ALU.mult, op1=ALU.max),
                                reads=["dff", "A_r", "ypa"], writes=wnm)

                    def emit_v_pair(n, FW=FW):
                        p2 = (n // 2) % 2
                        wb = wpa if p2 == 0 else cma
                        w2 = wb[:, 0: W513 + FW]
                        v2 = vza[:, 2 * p2 * W513: 2 * p2 * W513 + W513 + FW]
                        rnm = ["wp0", "wp1"] if p2 == 0 else ["cma", "cmash"]
                        wnm = (["vzaA", "vzaB"], ["vzaC", "vzaD"])[p2]
                        sc.emit("s", lambda s_, w=w2, vv=v2:
                                s_.activation(out=vv, in_=w, func=AF.Exp, scale=-1.0),
                                reads=rnm, writes=wnm)

                    def emit_z(n, FW=FW):
                        sl = 2 * ((n // 2) % 2) + (n % 2)
                        nm = ("vzaA", "vzaB", "vzaC", "vzaD")[sl]
                        vsl = vza[:, sl * W513: sl * W513 + FW]
                        sc.emit("v", lambda v, n=n, vv=vsl, fw=FW:
                                v.tensor_tensor(out=vv, in0=vv,
                                                in1=qxx[:, n * F: n * F + fw],
                                                op=ALU.mult),
                                reads=[nm, f"q{n}"], writes=[nm])

                    def emit_mm(n, dt=dt, FW=FW, acols=acols, pacc=pacc, pan=pan):
                        sl = 2 * ((n // 2) % 2) + (n % 2)
                        zn = ("vzaA", "vzaB", "vzaC", "vzaD")[sl]
                        zsl = vza[:, sl * W513: sl * W513 + FW]
                        pi_d = dt % 2
                        for ri, (a, b) in enumerate(acols):
                            pcol = a if ri == 0 else 512
                            sc.emit("t", lambda t, a=a, b=b, pc=pcol, n=n, z=zsl, p=pi_d, pa=pacc:
                                    t.matmul(out=pa[:, pc:pc + b - a],
                                             lhsT=diagbuf[:, (p * NN + n) * P:(p * NN + n + 1) * P],
                                             rhs=z[:, a:b],
                                             start=False, stop=(n == NN - 1)),
                                    reads=[zn] + ([f"diagb{pi_d}"] if n in (0, NN - 1) else []),
                                    writes=[pan] if (n == NN - 1 and ri == len(acols) - 1) else [])

                    emit_g(0)
                    emit_g(1)
                    for n0 in range(0, NN, 2):
                        emit_ypwp(n0)
                        if n0 + 2 < NN:
                            emit_g(n0 + 2)
                        emit_ypwp(n0 + 1)
                        if n0 + 3 < NN:
                            emit_g(n0 + 3)
                        emit_v_pair(n0)
                        if n0 == 0 and evac_prev[0] is not None:
                            evac_prev[0]()
                            evac_prev[0] = None
                        if n0 >= 2:
                            emit_z(n0 - 2)
                            emit_mm(n0 - 2)
                            emit_z(n0 - 1)
                            emit_mm(n0 - 1)
                    emit_z(NN - 2)
                    emit_mm(NN - 2)
                    emit_z(NN - 1)
                    emit_mm(NN - 1)
                    if dt + 2 < NDT:
                        load_diag(dt + 2)
                    def do_evac(dt=dt, FW=FW, pacc=pacc, pan=pan):
                        sc.emit("v", lambda v, d=dt, w=min(FW, 512), pa=pacc:
                                v.tensor_copy(out=accF[:, d * F: d * F + w],
                                              in_=pa[:, 0:w]),
                                reads=[pan], writes=["accF"])
                        if FW == 513:
                            sc.emit("v", lambda v, d=dt, pa=pacc:
                                    v.tensor_copy(out=accF[:, d * F + 512: d * F + 513],
                                                  in_=pa[:, 512:513]),
                                    reads=[pan], writes=["accF"])
                    if dt == NDT - 1:
                        do_evac()
                    else:
                        evac_prev[0] = do_evac

                for lt in range(4):
                    col = TKO + lt * P
                    ptr = ps_tr if lt % 2 == 0 else ps_acc
                    ptn = "ps_tr" if lt % 2 == 0 else "ps_acc"
                    for dt in range(NDT):
                        sc.emit("t", lambda t, d=dt, c=col, pa=ptr:
                                t.transpose(out=pa[:, d * P:(d + 1) * P],
                                            in_=accF[:, d * F + c: d * F + c + P],
                                            identity=identf[:]),
                                reads=["accF", "identf_r"], writes=[ptn])
                    sc.emit("v", lambda v, l=lt, pa=ptr:
                            v.scalar_tensor_tensor(out=xstage[:, l * D:(l + 1) * D],
                                                   in0=pa[:, 0:D], scalar=1.0,
                                                   in1=xstage[:, l * D:(l + 1) * D],
                                                   op0=ALU.mult, op1=ALU.add),
                            reads=[ptn, f"xst{lt}"], writes=[f"xst{lt}"])
                emit_stats_group([xstage[:, lt * D:(lt + 1) * D] for lt in range(4)],
                                 [[f"xst{lt}"] for lt in range(4)], gbuf[:, 0:1024])
                for lt in range(4):
                    emit_ln_apply(xstage[:, lt * D:(lt + 1) * D], lt,
                                  lnstage[:, 0:1024], [f"xst{lt}"], ["lnstage", "wp0"])
                    for dt in range(NDT):
                        sc.emit("t", lambda t, d=dt:
                                t.transpose(out=ps_bf[:, d * P:(d + 1) * P],
                                            in_=lnstage[:, d * P:(d + 1) * P],
                                            identity=identb[:]),
                                reads=["lnstage", "wp0", "identb_r"], writes=["ps_bf"])
                    lcol = 1 + lt * P
                    sc.emit("v", lambda v, c=lcol:
                            [v.tensor_copy(out=bf2a[:, d * F + c: d * F + c + P],
                                           in_=ps_bf[:, d * P:(d + 1) * P])
                             for d in range(NDT)][-1],
                            reads=["ps_bf"], writes=["bf2a"])
                if c0:
                    for dt in range(NDT):
                        sc.emit("t", lambda t, d=dt:
                                t.transpose(out=ps_tr[0:1, d * P:(d + 1) * P],
                                            in_=accF[:, d * F: d * F + 1],
                                            identity=identf[:]),
                                reads=["accF", "identf_r"], writes=["ps_tr"])
                    sc.emit("v", lambda v:
                            v.scalar_tensor_tensor(out=xhalo[:], in0=ps_tr[0:1, 0:D],
                                                   scalar=1.0, in1=xhalo[:],
                                                   op0=ALU.mult, op1=ALU.add),
                            reads=["ps_tr", "xhalo"], writes=["xhalo"])
                    emit_ln(xhalo[:], 1, lnhalo[0:1, 0:1024], [], gbuf[0:1, 0:1024],
                            hstats, "hstats", ["xhalo"], ["gbuf0", "gbuf1"], ["lnhalo", "wp1"])
                    sc.emit("v", lambda v:
                            v.tensor_scalar(out=lnhalo[0:1, 0:1024],
                                            in0=lnhalo[0:1, 0:1024],
                                            scalar1=maskp[0:1, 0:1], scalar2=None,
                                            op0=ALU.mult),
                            reads=["lnhalo", "maskv_r"], writes=["lnhalo", "wp1"])
                    for dt in range(NDT):
                        sc.emit("t", lambda t, d=dt:
                                t.transpose(out=ps_bf[0:P, 960 + 2 * d:961 + 2 * d],
                                            in_=lnhalo[0:1, d * P:(d + 1) * P],
                                            identity=identb[0:1, 0:1]),
                                reads=["lnhalo", "wp1", "identb_r"], writes=["ps_bf"])
                    sc.emit("v", lambda v:
                            [v.tensor_copy(out=bf2a[:, d * F: d * F + 1],
                                           in_=ps_bf[:, 960 + 2 * d:961 + 2 * d])
                             for d in range(NDT)][-1],
                            reads=["ps_bf"], writes=["bf2a"])
                else:
                    sc.emit("v", lambda v:
                            [v.tensor_copy(out=bf2a[:, d * F: d * F + 1],
                                           in_=carrycol[:, d:d + 1])
                             for d in range(NDT)][-1],
                            reads=["carry"], writes=["bf2a"])
                sc.emit("v", lambda v:
                        [v.tensor_copy(out=carrycol[:, d:d + 1],
                                       in_=bf2a[:, d * F + 512: d * F + 513])
                         for d in range(NDT)][-1],
                        reads=["bf2a"], writes=["carry"])

                for dt in range(NDT):
                    dc = dt * F
                    sc.emit("v", lambda v, d=dc:
                            v.tensor_tensor(out=gbuf[:, 0:512],
                                            in0=bf2a[:, d + 1: d + 513],
                                            in1=bf2a[:, d: d + 512], op=ALU.subtract),
                            reads=["bf2a"], writes=["gbuf0"])
                    sc.emit("v", lambda v, d=dt, dc=dc:
                            v.scalar_tensor_tensor(out=qxx[:, d * 512:(d + 1) * 512],
                                                   in0=gbuf[:, 0:512],
                                                   scalar=murs[:, d:d + 1],
                                                   in1=bf2a[:, dc: dc + 512],
                                                   op0=ALU.mult, op1=ALU.add),
                            reads=["gbuf0", "mur_r", "bf2a"], writes=["xr" + str(dt)])
                    sc.emit("v", lambda v, d=dt, dc=dc:
                            v.scalar_tensor_tensor(out=qxx[:, (8 + d) * 512:(9 + d) * 512],
                                                   in0=gbuf[:, 0:512],
                                                   scalar=muks[:, d:d + 1],
                                                   in1=bf2a[:, dc: dc + 512],
                                                   op0=ALU.mult, op1=ALU.add),
                            reads=["gbuf0", "muk_r", "bf2a"], writes=["xk" + str(dt)])
                for do in range(NDT):
                    pn = f"ps_mlp{pp[0]}"
                    mmacc(lambda i, d=do: wrtb[:, (i * 8 + d) * P:(i * 8 + d + 1) * P],
                          lambda i: qxx[:, i * 512:(i + 1) * 512],
                          NDT, pn, ps_mlp[pp[0]][:, 0:512],
                          [f"xr{i}" for i in range(NDT)] + ["w_WrT"])
                    sc.emit("s", lambda s_, d=do, p=pp[0]:
                            s_.activation(out=accF[:, d * F: d * F + 512],
                                          in_=ps_mlp[p][:, 0:512],
                                          func=AF.Tanh, scale=0.5),
                            reads=[pn], writes=["accF"])
                    sc.emit("v", lambda v, d=do:
                            v.tensor_scalar(out=accF[:, d * F: d * F + 512],
                                            in0=accF[:, d * F: d * F + 512],
                                            scalar1=0.5, scalar2=0.5,
                                            op0=ALU.mult, op1=ALU.add),
                            reads=["accF"], writes=["accF"])
                    pp[0] ^= 1
                for do in range(NDT):
                    pn = f"ps_mlp{pp[0]}"
                    mmacc(lambda i, d=do: wktb[:, (i * 8 + d) * P:(i * 8 + d + 1) * P],
                          lambda i: qxx[:, (8 + i) * 512:(9 + i) * 512],
                          NDT, pn, ps_mlp[pp[0]][:, 0:512],
                          [f"xk{i}" for i in range(NDT)] + ["w_WkT"])
                    sc.emit("s", lambda s_, p=pp[0]:
                            s_.activation(out=gbuf[:, 0:512], in_=ps_mlp[p][:, 0:512],
                                          func=AF.Identity),
                            reads=[pn], writes=["gbuf0"])
                    sc.emit("v", lambda v, d=do:
                            v.scalar_tensor_tensor(out=e2k[:, d * F: d * F + 512],
                                                   in0=gbuf[:, 0:512], scalar=0.0,
                                                   in1=gbuf[:, 0:512],
                                                   op0=ALU.max, op1=ALU.mult),
                            reads=["gbuf0"], writes=["e2k"])
                    pp[0] ^= 1
                for do in range(NDT):
                    pn = f"ps_mlp{pp[0]}"
                    mmacc(lambda i, d=do: wvtb[:, (i * 8 + d) * P:(i * 8 + d + 1) * P],
                          lambda i: e2k[:, i * F: i * F + 512],
                          NDT, pn, ps_mlp[pp[0]][:, 0:512], ["e2k", "w_WvT"])
                    sc.emit("v", lambda v, d=do, p=pp[0]:
                            v.tensor_tensor(out=dff[:, d * F: d * F + 512],
                                            in0=ps_mlp[p][:, 0:512],
                                            in1=accF[:, d * F: d * F + 512],
                                            op=ALU.mult),
                            reads=[pn, "accF"], writes=["dff"])
                    pp[0] ^= 1
                y2s = qxx.bitcast(F32)
                for lt in range(4):
                    ptr = ps_tr if lt % 2 == 0 else ps_acc
                    ptn = "ps_tr" if lt % 2 == 0 else "ps_acc"
                    for dt in range(NDT):
                        sc.emit("t", lambda t, d=dt, l=lt, pa=ptr:
                                t.transpose(out=pa[:, d * P:(d + 1) * P],
                                            in_=dff[:, d * F + l * P: d * F + l * P + P],
                                            identity=identf[:]),
                                reads=["dff", "identf_r"], writes=[ptn])
                    sc.emit("v", lambda v, l=lt, y2=y2s, pa=ptr:
                            v.scalar_tensor_tensor(out=y2[:, l * D:(l + 1) * D],
                                                   in0=pa[:, 0:D], scalar=1.0,
                                                   in1=xstage[:, l * D:(l + 1) * D],
                                                   op0=ALU.mult, op1=ALU.add),
                            reads=[ptn, f"xst{lt}"]
                                  + ([f"xr{i}" for i in range(4 * lt, 4 * lt + 4)]
                                     if lt < 2 else
                                     [f"xk{i}" for i in range(4 * (lt - 2), 4 * (lt - 2) + 4)])
                                  + [f"q{n}" for n in range(max(0, 4 * lt - 1),
                                                            min(NN, 4 * lt + 4))],
                            writes=[f"y2s{lt}"])
                    sc.emit("g", lambda g, l=lt, r=row0, y2=y2s:
                            g.dma_start(out=ys[r - 1 + l * P: r - 1 + (l + 1) * P, :],
                                        in_=y2[:, l * D:(l + 1) * D]),
                            reads=[f"y2s{lt}"], writes=[f"ysout{ci}_{lt}"], dma=True)

                obar = sc.dma_barrier(
                    "g",
                    scrub=[f"y2s{l_}" for l_ in range(4)]
                          + [f"xr{i_}" for i_ in range(NDT)]
                          + [f"xk{i_}" for i_ in range(NDT)]
                          + [f"q{n_}" for n_ in range(NN)])

            fin = sc.tok("gd")
            sc.emit("g", lambda g: g.engine_nop(), waits=[fin])

            sc.check_dma_waits()

            @block.vector
            def _(v):
                sc.replay("v", v, sems)

            @block.scalar
            def _(s_):
                sc.replay("s", s_, sems)

            @block.tensor
            def _(t):
                sc.replay("t", t, sems)

            @block.gpsimd
            def _(g):
                sc.replay("g", g, sems)

            @block.sync
            def _(q):
                sc.replay("q", q, sems)

    return nc


_NC_CACHE = []


def kernel(**inputs):
    x = np.ascontiguousarray(inputs["L0_attn_x"], dtype=np.float32)
    B, L, _ = x.shape
    per = L // 4

    def f32(a):
        return np.ascontiguousarray(a, dtype=np.float32)

    weights = {
        "Wl1": f32(inputs["L0_attn_Wl1"]),
        "Wl2": f32(inputs["L0_attn_Wl2"]),
        "Wf1": f32(inputs["L0_attn_Wf1"]),
        "Wf2": f32(inputs["L0_attn_Wf2"]),
        "W1": f32(inputs["L0_attn_W1"]),
        "W23": np.ascontiguousarray(
            np.concatenate([f32(inputs["L0_attn_W2"]), f32(inputs["L0_attn_W3"])],
                           axis=1)),
        "WrT": np.ascontiguousarray(f32(inputs["L0_ffn_W_r"]).T),
        "WkT": np.ascontiguousarray(f32(inputs["L0_ffn_W_k"]).T),
        "WvT": np.ascontiguousarray(f32(inputs["L0_ffn_W_v"]).T),
        "A": f32(inputs["L0_attn_A"]),
        "S": f32(inputs["L0_attn_state_init"]).reshape(D, NN),
        "bl1": f32(inputs["L0_attn_bl1"]).reshape(P, 1),
        "bl2": f32(inputs["L0_attn_bl2"]).reshape(D, 1),
        "bf1": f32(inputs["L0_attn_bf1"]).reshape(P, 1),
        "bf2": f32(inputs["L0_attn_bf2"]).reshape(D, 1),
        "b1": f32(inputs["L0_attn_b1"]).reshape(D, 1),
        "b23": np.ascontiguousarray(
            np.concatenate([f32(inputs["L0_attn_b2"]),
                            f32(inputs["L0_attn_b3"])]).reshape(32, 1)),
        "mur": f32(inputs["L0_ffn_mu_r"]).reshape(D, 1),
        "muk": f32(inputs["L0_ffn_mu_k"]).reshape(D, 1),
        "identf": np.eye(P, dtype=np.float32),
        "identb": np.eye(P).astype(BFDT),
    }

    if not _NC_CACHE:
        _NC_CACHE.append(build_program())
    nc = _NC_CACHE[0]

    in_maps = []
    for c in range(8):
        b, s = divmod(c, 4)
        t0 = s * per
        shard = np.zeros((ROWS, D), dtype=np.float32)
        shard[1:] = x[b, t0:t0 + per]
        if t0 > 0:
            shard[0] = x[b, t0 - 1]
        im = {"xs": shard,
              "maskv": np.full((P, 1), 0.0 if t0 == 0 else 1.0, dtype=np.float32)}
        im.update(weights)
        in_maps.append(im)

    res = run_bass_kernel_spmd(nc, in_maps, core_ids=list(range(8)))
    out = np.empty((B, L, D), dtype=np.float32)
    for c in range(8):
        b, s = divmod(c, 4)
        t0 = s * per
        out[b, t0:t0 + per] = res.results[c]["ys"]
    return out


# version 21
# speedup vs baseline: 1.1905x; 1.1905x over previous
import contextlib
import os
import numpy as np
import ml_dtypes
import concourse.bass as bass
import concourse.mybir as mybir
from concourse.bass_utils import run_bass_kernel_spmd


F32 = mybir.dt.float32
BF16 = mybir.dt.bfloat16
AF = mybir.ActivationFunctionType
ALU = mybir.AluOpType
AX = mybir.AxisListType

P = 128
D = 1024
NDT = 8
NN = 16
TOK = 2048
ROWS = TOK + 1
EPS = 1e-5
W513 = 514

BFDT = ml_dtypes.bfloat16


class Sched:

    STREAMS = ("v", "s", "t", "g", "q")
    SEMS = ("v", "s", "t", "g", "q", "sq", "qd", "gd")
    DMA_SEM = {"q": "qd", "g": "gd", "s": "sq"}

    def __init__(self):
        self.ops = {e: [] for e in self.STREAMS}
        self.cnt = {e: 0 for e in self.SEMS}
        self.writer = {}
        self.readers = {}
        self.bar_vals = {s: {0} for s in self.SEMS}
        self.dbg_rw = []

    def emit(self, eng, fn, reads=(), writes=(), waits=(), dma=False, sem=None,
             hard_waits=()):
        if dma:
            sem = sem or self.DMA_SEM[eng]
        else:
            sem = sem or eng
        amt = 16 if dma else 1
        wl = list(waits)
        for b in reads:
            if b in self.writer:
                wl.append(self.writer[b])
        for b in writes:
            if b in self.writer:
                wl.append(self.writer[b])
            wl.extend(self.readers.get(b, ()))
        wmax = {}
        keep_same = os.environ.get("BASS_SIM_FP64") == "1"
        for (s, val) in wl:
            if s == eng and not keep_same:
                continue
            if val > wmax.get(s, -1):
                wmax[s] = val
        for (s, val) in hard_waits:
            if val > wmax.get(s, -1):
                wmax[s] = val
        self.cnt[sem] += amt
        tok = (sem, self.cnt[sem])
        self.ops[eng].append((fn, sorted(wmax.items()), sem, amt, (tuple(reads), tuple(writes))))
        self.dbg_rw.append((eng, sorted(wmax.items()), tuple(reads), tuple(writes)))
        for b in writes:
            self.writer[b] = tok
            self.readers[b] = []
        for b in reads:
            self.readers.setdefault(b, []).append(tok)
        return tok

    def tok(self, sem):
        return (sem, self.cnt[sem])

    DMA_SEMS = ("qd", "gd", "sq")

    def dma_barrier(self, eng, rebind=(), scrub=()):
        dsem = self.DMA_SEM[eng]
        tgt = self.cnt[dsem]
        self.bar_vals[dsem].add(tgt)
        tok = self.emit(eng, lambda e: e.nop(), waits=[(dsem, tgt)])
        for b in rebind:
            if b in self.writer:
                self.writer[b] = tok
        for b in scrub:
            if b in self.readers:
                self.readers[b] = [t for t in self.readers[b]
                                   if t[0] not in self.DMA_SEMS] + [tok]
        return tok

    def check_dma_waits(self):
        import sys
        bad = 0
        for eng, waits, rd, wr in self.dbg_rw:
            for s, val in waits:
                if s in self.DMA_SEMS and val not in self.bar_vals[s]:
                    print(f"BAD DMA WAIT eng={eng} {s}>={val} reads={rd} writes={wr}",
                          file=sys.stderr)
                    bad += 1
                    if bad > 10:
                        return
    def replay(self, eng, eng_obj, sems):
        last = {}
        for idx, (fn, waits, sem, amt, dbg) in enumerate(self.ops[eng]):
            for s, val in waits:
                if last.get(s, -1) >= val:
                    continue
                eng_obj.wait_ge(sems[s], val)
                last[s] = val
            try:
                inst = fn(eng_obj)
            except Exception:
                print(f"REPLAY FAIL eng={eng} idx={idx} rw={dbg}")
                raise
            inst.then_inc(sems[sem], amt)


def build_program():
    nc = bass.Bass("TRN2", target_bir_lowering=False)

    xs = nc.dram_tensor("xs", [ROWS, D], F32, kind="ExternalInput").ap()
    ys = nc.dram_tensor("ys", [TOK, D], F32, kind="ExternalOutput").ap()
    win = {}
    for name, shape, dt in [
        ("Wl1", [D, P], F32), ("Wl2", [P, D], F32), ("Wf1", [2 * D, P], F32),
        ("Wf2", [P, D], F32), ("W1", [D, D], F32), ("W23", [D, 32], F32),
        ("WrT", [D, D], F32), ("WkT", [D, D], F32), ("WvT", [D, D], F32),
        ("A", [D, NN], F32), ("S", [D, NN], F32),
        ("bl1", [P, 1], F32), ("bl2", [D, 1], F32), ("bf1", [P, 1], F32),
        ("bf2", [D, 1], F32), ("b1", [D, 1], F32), ("b23", [32, 1], F32),
        ("mur", [D, 1], F32), ("muk", [D, 1], F32),
        ("identf", [P, P], F32), ("identb", [P, P], BF16), ("maskv", [P, 1], F32),
    ]:
        win[name] = nc.dram_tensor(name, shape, dt, kind="ExternalInput").ap()
    qscr = nc.dram_tensor("qscr", [NN, W513], BF16, kind="ExternalOutput").ap()
    cbscr = nc.dram_tensor("cbscr", [1, W513], F32, kind="ExternalOutput").ap()
    diagscr = nc.dram_tensor("diagscr", [P, NDT * NN * P], BF16, kind="ExternalOutput").ap()

    es = contextlib.ExitStack()
    sb = lambda n, s, d=F32: es.enter_context(nc.sbuf_tensor(n, s, d))
    psb = lambda n, s, d=F32: es.enter_context(nc.psum_tensor(n, s, d))

    with es:
        w1b = sb("w1b", [P, 64 * P], BF16)
        wrtb = sb("wrtb", [P, 64 * P], BF16)
        wktb = sb("wktb", [P, 64 * P], BF16)
        wvtb = sb("wvtb", [P, 64 * P], BF16)
        wf1b = sb("wf1b", [P, 16 * P], BF16)
        wl1b = sb("wl1b", [P, 8 * P], BF16)
        wl2b = sb("wl2b", [P, 8 * P], BF16)
        wf2b = sb("wf2b", [P, 8 * P], BF16)
        w23b = sb("w23b", [P, 8 * 32], BF16)
        A_sb = sb("A_sb", [P, NDT * NN])
        S_sb = sb("S_sb", [P, NDT * NN])
        identb = sb("identb_s", [P, P], BF16)
        identf = sb("identf_s", [P, P])
        onesf = sb("onesf", [16, 1])
        bl1 = sb("bl1_s", [P, 1])
        bl2 = sb("bl2_s", [P, NDT])
        bf1 = sb("bf1_s", [P, 1])
        bf2 = sb("bf2_s", [P, NDT])
        b1s = sb("b1_s", [P, NDT])
        b23 = sb("b23_s", [32, 1])
        murs = sb("mur_s", [P, NDT])
        muks = sb("muk_s", [P, NDT])
        maskp = sb("mask_s", [P, 1])
        ceps = sb("ceps", [P, 1])
        cone = sb("cone", [P, 1])
        carrycol = sb("carrycol", [P, NDT], BF16)

        xstage = sb("xstage", [P, 4 * D])
        xhalo = sb("xhalo", [1, D])
        accF = sb("accF", [P, 8 * W513])
        dff = sb("dff", [P, 8 * W513])
        qxx = sb("qxx", [P, NN * W513], BF16)
        bf1a = sb("bf1a", [P, 8 * W513], BF16)
        bf2a = sb("bf2a", [P, 8 * W513], BF16)
        e2k = sb("e2k", [P, 8 * W513], BF16)
        gbuf = sb("gbuf", [P, 2 * W513])
        ypa = sb("ypa", [P, W513])
        wpa = sb("wpa", [P, 2 * W513])
        vza = sb("vza", [P, 4 * W513], BF16)
        cma = sb("cma", [P, 2 * W513])
        cmbf = sb("cmbf", [16, W513], BF16)
        diagbuf = sb("diagbuf", [P, 2 * NN * P], BF16)
        stats = sb("stats", [P, 24])
        hstats = sb("hstats", [1, 8])

        lnstage = wpa[:, 0:512].bitcast(BF16)
        lnhalo = wpa[0:1, 513:1025].bitcast(BF16)

        ps_mlp = [psb("ps_mlp0", [P, 512]), psb("ps_mlp1", [P, 512])]
        ps_acc = psb("ps_acc", [P, 1024])
        ps_tr = psb("ps_tr", [P, 1024])
        ps_bf = psb("ps_bf", [P, 1024], BF16)
        ps_sm = psb("ps_sm", [P, 512])

        nc.const_aps.register(F32, EPS, ceps[:])
        nc.const_aps.register(F32, 1.0, cone[:])

        sems = {}
        with contextlib.ExitStack() as ses:
            for s in Sched.SEMS:
                sems[s] = ses.enter_context(nc.semaphore(f"sem_{s}"))
            block = ses.enter_context(nc.Block())
            sc = Sched()

            sc.emit("v", lambda v: v.memset(ceps[:], EPS), writes=["ceps"])
            sc.emit("v", lambda v: v.memset(cone[:], 1.0), writes=["cone"])
            sc.emit("v", lambda v: v.memset(onesf[:], 1.0), writes=["onesf"])
            sc.emit("v", lambda v: v.memset(qxx[:], 0.0),
                    writes=[f"q{n}" for n in range(NN)]
                           + [f"xr{i}" for i in range(NDT)]
                           + [f"xk{i}" for i in range(NDT)]
                           + [f"y2s{l}" for l in range(4)]
                           + [f"slot{k}" for k in range(8, 12)])
            sc.emit("v", lambda v: v.memset(wpa[:, 512:514], 0.0),
                    writes=["wp0"])

            def stage_cast(nm_src_ap, dst_ap, slot_name, slot_ap):
                t_l = sc.emit("q", lambda q, o=slot_ap, i=nm_src_ap:
                              q.dma_start(out=o, in_=i),
                              writes=[slot_name], dma=True)
                return sc.emit("v", lambda v, s=slot_ap, d=dst_ap:
                               v.tensor_copy(out=d, in_=s),
                               reads=[slot_name], waits=[t_l])

            for lt in range(4):
                sc.emit("q", lambda q, l=lt:
                        q.dma_start(out=xstage[:, l * D:(l + 1) * D],
                                    in_=xs[1 + l * P: 1 + (l + 1) * P, :]),
                        writes=[f"xst{lt}"], dma=True)
            sc.emit("q", lambda q: q.dma_start(out=xhalo[:], in_=xs[0:1, :]),
                    writes=["xhalo"], dma=True)
            sc.dma_barrier("q", rebind=[f"xst{l}" for l in range(4)] + ["xhalo"])

            for nm, dst, rearr in [
                ("A", A_sb, True), ("S", S_sb, True), ("bl2", bl2, True),
                ("bf2", bf2, True), ("b1", b1s, True), ("mur", murs, True),
                ("muk", muks, True),
                ("identb", identb, False), ("identf", identf, False),
                ("bl1", bl1, False), ("bf1", bf1, False), ("b23", b23, False),
                ("maskv", maskp, False),
            ]:
                src2 = (win[nm].rearrange("(a p) c -> p a c", p=P) if rearr
                        else win[nm])
                sc.emit("q", lambda q, o=dst, i=src2:
                        q.dma_start(out=o[:], in_=i,
                                    allow_slow_non_contiguous=True),
                        writes=[nm + "_r"], dma=True)
            sc.dma_barrier("q", rebind=[nm_ + "_r" for nm_ in (
                "A", "S", "bl2", "bf2", "b1", "mur", "muk", "identb", "identf",
                "bl1", "bf1", "b23", "maskv")])

            jobs = []
            for nm, dstb, is_tiled in [("Wl1", wl1b, True), ("Wl2", wl2b, False),
                                       ("Wf2", wf2b, False)]:
                s_ = (win[nm].rearrange("(a p) c -> p a c", p=P) if is_tiled
                      else win[nm])
                jobs.append((s_, dstb[:, 0:1024], 1024, "w_" + nm))
            for half in range(2):
                jobs.append((win["Wf1"][half * 1024:(half + 1) * 1024, :]
                             .rearrange("(a p) c -> p a c", p=P),
                             wf1b[:, half * 1024:(half + 1) * 1024], 1024, "w_Wf1"))
            jobs.append((win["W23"].rearrange("(a p) c -> p a c", p=P),
                         w23b[:, 0:256], 256, "w_W23"))
            big = [("W1", w1b), ("WrT", wrtb), ("WkT", wktb), ("WvT", wvtb)]
            for nm, dstb in big:
                for dt in range(NDT):
                    jobs.append((win[nm][dt * P:(dt + 1) * P, :],
                                 dstb[:, dt * 1024:(dt + 1) * 1024], 1024, "w_" + nm))

            qxxf = qxx.bitcast(F32)
            bf2f = bf2a.bitcast(F32)
            e2kf = e2k.bitcast(F32)
            vzaf = vza.bitcast(F32)
            dgbf = diagbuf.bitcast(F32)
            slots = ([("accF", accF[:, i * 1024:(i + 1) * 1024]) for i in range(4)]
                     + [("dff", dff[:, i * 1024:(i + 1) * 1024]) for i in range(4)]
                     + [("qxx", qxxf[:, i * 1024:(i + 1) * 1024]) for i in range(4)]
                     + [("bf2a", bf2f[:, i * 1024:(i + 1) * 1024]) for i in range(2)]
                     + [("e2k", e2kf[:, i * 1024:(i + 1) * 1024]) for i in range(2)]
                     + [("dgb", dgbf[:, i * 1024:(i + 1) * 1024]) for i in range(2)]
                     + [("vza", vzaf[:, 0:1024]), ("cma", cma[:, 0:1024])])
            SLOT_ARENAS = sorted({sn for sn, _ in slots})

            def emit_round(grp, off):
                bars = []
                for k, (s_, d_, w_, wn_) in enumerate(grp):
                    qq = ("q", "g", "s")[k % 3]
                    sn, sap = slots[off + k]
                    sc.emit(qq, lambda q, o=sap, i=s_, w=w_:
                            q.dma_start(out=o[:, 0:w] if w != 1024 else o, in_=i),
                            writes=[f"slot{off + k}"], dma=True)
                sl = [f"slot{off + k}" for k in range(len(grp))]
                bars.append(sc.dma_barrier("q", rebind=sl))
                bars.append(sc.dma_barrier("g", rebind=sl))
                bars.append(sc.dma_barrier("s", rebind=sl))
                toks = []
                for k, (s_, d_, w_, wn_) in enumerate(grp):
                    sn, sap = slots[off + k]
                    toks.append(sc.emit("v", lambda v, s2=sap, d=d_, w=w_:
                                        v.tensor_copy(out=d, in_=s2[:, 0:w] if w != 1024 else s2),
                                        reads=[f"slot{off + k}"], writes=[wn_],
                                        waits=bars))
                return toks

            r1 = jobs[0:20]
            r2 = jobs[20:]
            for k, (s_, d_, w_, wn_) in enumerate(r1):
                qq = "q" if k % 2 == 0 else "g"
                sn, sap = slots[k]
                sc.emit(qq, lambda q, o=sap, i=s_, w=w_:
                        q.dma_start(out=o[:, 0:w] if w != 1024 else o, in_=i),
                        writes=[f"slot{k}"], dma=True)
            _sl1 = [f"slot{k}" for k in range(len(r1))]
            r1bars = [sc.dma_barrier("q", rebind=_sl1),
                      sc.dma_barrier("g", rebind=_sl1)]

            def emit_deferred_init():
                toks = []
                for k, (s_, d_, w_, wn_) in enumerate(r1):
                    sn, sap = slots[k]
                    toks.append(sc.emit("v", lambda v, s2=sap, d=d_, w=w_:
                                        v.tensor_copy(out=d, in_=s2[:, 0:w] if w != 1024 else s2),
                                        reads=[f"slot{k}"], writes=[wn_],
                                        waits=r1bars))
                toks.extend(emit_round(r2, 0))
                for tok in toks:
                    for an in SLOT_ARENAS:
                        sc.readers.setdefault(an, []).append(tok)
                for tok in toks:
                    for i in range(NDT):
                        sc.readers.setdefault(f"xr{i}", []).append(tok)
                        sc.readers.setdefault(f"xk{i}", []).append(tok)
                    for an in ("vzaA", "vzaB", "vzaC", "vzaD", "ypa",
                               "cma", "cmash", "diagb0", "diagb1"):
                        sc.readers.setdefault(an, []).append(tok)
                    for n in range(NN):
                        sc.readers.setdefault(f"q{n}", []).append(tok)
                for dt_ in range(NDT):
                    pi = dt_ % 2
                    for n_ in range(NN):
                        k = dt_ * NN + n_
                        sc.emit("v", lambda v, k=k, p=pi, n=n_:
                                v.tensor_scalar(out=diagbuf[:, (p * NN + n) * P:(p * NN + n + 1) * P],
                                                in0=identf[:],
                                                scalar1=S_sb[:, k:k + 1], scalar2=None,
                                                op0=ALU.mult),
                                reads=["identf_r", "S_r"],
                                writes=[f"diagb{pi}"] if n_ == 0 else [])
                    sc.writer[f"diagb{pi}"] = sc.tok("v")
                    sc.emit("g", lambda g, d=dt_, p=pi:
                            g.dma_start(out=diagscr[:, d * NN * P:(d + 1) * NN * P],
                                        in_=diagbuf[:, p * NN * P:(p + 1) * NN * P]),
                            reads=[f"diagb{pi}"], writes=["diagscr_d"], dma=True)
                    sc.dma_barrier("g", rebind=["diagscr_d"],
                                   scrub=[f"diagb{pi}"])
                return toks

            def emit_ln(src_ap, rows, dst_ap, extra_waits, scr_ap, st, st_name,
                        src_bufs, scr_bufs, dst_bufs):
                r = rows
                t1 = sc.emit("v", lambda v: v.reduce_sum(out=st[0:r, 0:1], in_=src_ap,
                                                         axis=AX.X),
                             reads=src_bufs, writes=[st_name], waits=extra_waits)
                t2a = sc.emit("v", lambda v: v.scalar_tensor_tensor(out=scr_ap, in0=src_ap,
                                                              scalar=1.0, in1=src_ap,
                                                              op0=ALU.mult, op1=ALU.mult,
                                                              accum_out=st[0:r, 1:2]),
                        reads=src_bufs, writes=scr_bufs + [st_name])
                sc.emit("v", lambda v: v.tensor_scalar(out=st[0:r, 2:3], in0=st[0:r, 0:1],
                                                       scalar1=1.0 / D, scalar2=None,
                                                       op0=ALU.mult),
                        reads=[st_name], writes=[st_name], hard_waits=[t1])
                sc.emit("v", lambda v: v.tensor_scalar(out=st[0:r, 3:4], in0=st[0:r, 1:2],
                                                       scalar1=1.0 / D, scalar2=None,
                                                       op0=ALU.mult),
                        reads=[st_name], writes=[st_name], hard_waits=[t2a])
                sc.emit("v", lambda v: v.scalar_tensor_tensor(out=st[0:r, 3:4],
                                                              in0=st[0:r, 2:3],
                                                              scalar=st[0:r, 2:3],
                                                              in1=st[0:r, 3:4],
                                                              op0=ALU.mult,
                                                              op1=ALU.subtract),
                        reads=[st_name], writes=[st_name])
                sc.emit("s", lambda s_: s_.activation(out=st[0:r, 4:5], in_=st[0:r, 3:4],
                                                      func=AF.Ln, bias=EPS, scale=-1.0),
                        reads=[st_name, "ceps"], writes=[st_name])
                sc.emit("s", lambda s_: s_.activation(out=st[0:r, 4:5], in_=st[0:r, 4:5],
                                                      func=AF.Exp, scale=-0.5),
                        reads=[st_name], writes=[st_name])
                return sc.emit("v", lambda v: v.tensor_scalar(out=dst_ap, in0=src_ap,
                                                              scalar1=st[0:r, 2:3],
                                                              scalar2=st[0:r, 4:5],
                                                              op0=ALU.subtract,
                                                              op1=ALU.mult),
                               reads=src_bufs + [st_name], writes=dst_bufs)

            def emit_stats_group(srcs, src_bufs_list, scr_ap):
                nt = len(srcs)
                for i, (src_ap, rb) in enumerate(zip(srcs, src_bufs_list)):
                    t_red = sc.emit("v", lambda v, s=src_ap, i=i:
                            v.reduce_sum(out=stats[:, i:i + 1], in_=s, axis=AX.X),
                            reads=rb, writes=["stats"])
                    t_acc = sc.emit("v", lambda v, s=src_ap, i=i, scr=scr_ap:
                            v.scalar_tensor_tensor(out=scr, in0=s, scalar=1.0, in1=s,
                                                   op0=ALU.mult, op1=ALU.mult,
                                                   accum_out=stats[:, 4 + i:5 + i]),
                            reads=rb, writes=["gbuf0", "gbuf1", "stats"])
                    sc.emit("v", lambda v, i=i:
                            v.tensor_scalar(out=stats[:, 8 + i:9 + i],
                                            in0=stats[:, i:i + 1],
                                            scalar1=1.0 / D, scalar2=None,
                                            op0=ALU.mult),
                            reads=["stats"], writes=["stats"], hard_waits=[t_red])
                    sc.emit("v", lambda v, i=i:
                            v.tensor_scalar(out=stats[:, 12 + i:13 + i],
                                            in0=stats[:, 4 + i:5 + i],
                                            scalar1=1.0 / D, scalar2=None,
                                            op0=ALU.mult),
                            reads=["stats"], writes=["stats"], hard_waits=[t_acc])
                    sc.emit("v", lambda v, i=i:
                            v.scalar_tensor_tensor(out=stats[:, 12 + i:13 + i],
                                                   in0=stats[:, 8 + i:9 + i],
                                                   scalar=stats[:, 8 + i:9 + i],
                                                   in1=stats[:, 12 + i:13 + i],
                                                   op0=ALU.mult, op1=ALU.subtract),
                            reads=["stats"], writes=["stats"])
                sc.emit("s", lambda s_, n=nt:
                        s_.activation(out=stats[:, 12:12 + n], in_=stats[:, 12:12 + n],
                                      func=AF.Ln, bias=EPS, scale=-1.0),
                        reads=["stats", "ceps"], writes=["stats"])
                sc.emit("s", lambda s_, n=nt:
                        s_.activation(out=stats[:, 12:12 + n], in_=stats[:, 12:12 + n],
                                      func=AF.Exp, scale=-0.5),
                        reads=["stats"], writes=["stats"])
                sc.emit("v", lambda v, n=nt:
                        v.scalar_tensor_tensor(out=stats[:, 16:16 + n],
                                               in0=stats[:, 8:8 + n], scalar=-1.0,
                                               in1=stats[:, 12:12 + n],
                                               op0=ALU.mult, op1=ALU.mult),
                        reads=["stats"], writes=["stats"])

            def emit_ln_apply(src_ap, i, dst_ap, src_bufs, dst_bufs):
                return sc.emit("s", lambda s_, s=src_ap, d=dst_ap, i=i:
                               s_.activation(out=d, in_=s, func=AF.Identity,
                                             scale=stats[:, 12 + i:13 + i],
                                             bias=stats[:, 16 + i:17 + i]),
                               reads=src_bufs + ["stats"], writes=dst_bufs)

            F = W513

            for ci in range(4):
                c0 = ci == 0
                FW = 513 if c0 else 512
                acols = [(0, 512), (512, 513)] if c0 else [(0, 512)]
                row0 = ci * 512 + 1
                LO = 1 if c0 else 0

                TKO = 1 if c0 else 0

                if not c0:
                    for lt in range(4):
                        qq = "q" if lt % 2 == 0 else "s"
                        sc.emit(qq, lambda q, l=lt, r=row0:
                                q.dma_start(out=xstage[:, l * D:(l + 1) * D],
                                            in_=xs[r + l * P: r + (l + 1) * P, :]),
                                writes=[f"xst{lt}"], dma=True)
                        sc.dma_barrier(qq, rebind=[f"xst{lt}"])

                ln1_parts = []
                emit_stats_group([xstage[:, lt * D:(lt + 1) * D] for lt in range(4)],
                                 [[f"xst{lt}"] for lt in range(4)], gbuf[:, 0:1024])
                for lt in range(4):
                    src = xstage[:, lt * D:(lt + 1) * D]
                    emit_ln_apply(src, lt, lnstage[:, 0:1024], [f"xst{lt}"],
                                  ["lnstage", "wp0"])
                    for dt in range(NDT):
                        sc.emit("t", lambda t, d=dt:
                                t.transpose(out=ps_bf[:, d * P:(d + 1) * P],
                                            in_=lnstage[:, d * P:(d + 1) * P],
                                            identity=identb[:]),
                                reads=["lnstage", "wp0", "identb_r"], writes=["ps_bf"])
                    col = TKO + lt * P
                    ln1_parts.append(
                        sc.emit("v", lambda v, c=col:
                                [v.tensor_copy(out=bf1a[:, d * F + c: d * F + c + P],
                                               in_=ps_bf[:, d * P:(d + 1) * P])
                                 for d in range(NDT)][-1],
                                reads=["ps_bf"], writes=["bf1a"]))
                if c0:
                    tlnh = emit_ln(xhalo[:], 1, lnhalo[0:1, 0:1024], [],
                                   gbuf[0:1, 0:1024], hstats, "hstats",
                                   ["xhalo"], ["gbuf0", "gbuf1"], ["lnhalo", "wp1"])
                    for dt in range(NDT):
                        sc.emit("t", lambda t, d=dt:
                                t.transpose(out=ps_bf[0:P, 960 + 2 * d:961 + 2 * d],
                                            in_=lnhalo[0:1, d * P:(d + 1) * P],
                                            identity=identb[0:1, 0:1]),
                                reads=["lnhalo", "wp1", "identb_r"], writes=["ps_bf"])
                    ln1_parts.append(
                        sc.emit("v", lambda v:
                                [v.tensor_copy(out=bf1a[:, d * F: d * F + 1],
                                               in_=ps_bf[:, 960 + 2 * d:961 + 2 * d])
                                 for d in range(NDT)][-1],
                                reads=["ps_bf"], writes=["bf1a"]))

                if c0:
                    emit_deferred_init()
                pp = [0]

                def mmacc(stat_fn, mov_fn, n_acc, psname, ps_ap, extra_reads):
                    toks = []
                    for i in range(n_acc):
                        toks.append(sc.emit(
                            "t", lambda t, s=stat_fn(i), m=mov_fn(i), i=i, n=n_acc,
                            o=ps_ap: t.matmul(out=o, lhsT=s, rhs=m, start=(i == 0),
                                              stop=(i == n - 1)),
                            reads=extra_reads if i == 0 else [],
                            writes=[psname] if i == 0 else []))
                    sc.writer[psname] = toks[-1]
                    return toks[-1]

                h1 = vza[:, 0:W513]
                for (a, b) in acols:
                    pn = f"ps_mlp{pp[0]}"
                    mmacc(lambda i: wl1b[:, i * P:(i + 1) * P],
                          lambda i, a=a, b=b: bf1a[:, i * F + a: i * F + b],
                          NDT, pn, ps_mlp[pp[0]][:, 0:b - a], ["bf1a", "w_Wl1"])
                    sc.emit("s", lambda s_, a=a, b=b, p=pp[0]:
                            s_.activation(out=h1[:, a:b], in_=ps_mlp[p][:, 0:b - a],
                                          func=AF.Relu, bias=bl1[:, 0:1]),
                            reads=[pn, "bl1_r"], writes=["vzaA"])
                    pp[0] ^= 1
                for do in range(NDT):
                    for (a, b) in acols:
                        pn = f"ps_mlp{pp[0]}"
                        sc.emit("t", lambda t, d=do, a=a, b=b, p=pp[0]:
                                t.matmul(out=ps_mlp[p][:, 0:b - a],
                                         lhsT=wl2b[:, d * P:(d + 1) * P],
                                         rhs=h1[:, a:b], start=True, stop=True),
                                reads=["vzaA", "w_Wl2"], writes=[pn])
                        sc.emit("s", lambda s_, d=do, a=a, b=b, p=pp[0]:
                                s_.activation(out=bf2a[:, d * F + a: d * F + b],
                                              in_=ps_mlp[p][:, 0:b - a],
                                              func=AF.Tanh, bias=bl2[:, d:d + 1]),
                                reads=[pn, "bl2_r"], writes=["bf2a"])
                        pp[0] ^= 1
                f1 = vza[:, 2 * W513:3 * W513]
                for (a, b) in acols:
                    def mov16(i, a=a, b=b):
                        if i < 8:
                            return bf1a[:, i * F + a: i * F + b]
                        return bf2a[:, (i - 8) * F + a: (i - 8) * F + b]
                    pn = f"ps_mlp{pp[0]}"
                    mmacc(lambda i: wf1b[:, i * P:(i + 1) * P], mov16, 16,
                          pn, ps_mlp[pp[0]][:, 0:b - a], ["bf1a", "bf2a", "w_Wf1"])
                    sc.emit("s", lambda s_, a=a, b=b, p=pp[0]:
                            s_.activation(out=f1[:, a:b], in_=ps_mlp[p][:, 0:b - a],
                                          func=AF.Relu, bias=bf1[:, 0:1]),
                            reads=[pn, "bf1_r"], writes=["vzaC"])
                    pp[0] ^= 1
                for do in range(NDT):
                    for (a, b) in acols:
                        pn = f"ps_mlp{pp[0]}"
                        sc.emit("t", lambda t, d=do, a=a, b=b, p=pp[0]:
                                t.matmul(out=ps_mlp[p][:, 0:b - a],
                                         lhsT=wf2b[:, d * P:(d + 1) * P],
                                         rhs=f1[:, a:b], start=True, stop=True),
                                reads=["vzaC", "w_Wf2"], writes=[pn])
                        sc.emit("s", lambda s_, d=do, a=a, b=b, p=pp[0]:
                                s_.activation(out=accF[:, d * F + a: d * F + b],
                                              in_=ps_mlp[p][:, 0:b - a],
                                              func=AF.Identity, bias=bf2[:, d:d + 1]),
                                reads=[pn, "bf2_r"], writes=["accF"])
                        pp[0] ^= 1
                for do in range(NDT):
                    sc.emit("v", lambda v, d=do, fw=FW:
                            v.tensor_copy(out=bf1a[:, d * F: d * F + fw],
                                          in_=accF[:, d * F: d * F + fw]),
                            reads=["accF"], writes=["bf1a"])
                for ri, (a, b) in enumerate(acols):
                    po = 0 if ri == 0 else 256
                    mmacc(lambda i: w23b[:, i * 32:(i + 1) * 32],
                          lambda i, a=a, b=b: bf1a[:, i * F + a: i * F + b],
                          NDT, "ps_sm", ps_sm[0:32, po:po + b - a], ["bf1a", "w_W23"])
                    sc.emit("s", lambda s_, a=a, b=b, po=po:
                            s_.activation(out=cma[0:32, a:b], in_=ps_sm[0:32, po:po + b - a],
                                          func=AF.Identity, bias=b23[:, 0:1]),
                            reads=["ps_sm", "b23_r"], writes=["cma"])
                sc.emit("g", lambda g, fw=FW: g.dma_start(out=cma[0:16, W513:W513 + fw],
                                                   in_=cma[16:32, 0:fw]),
                        reads=["cma"], writes=["cmash"], dma=True)
                sc.dma_barrier("g", rebind=["cmash"], scrub=["cma"])
                sc.emit("v", lambda v, fw=FW: v.tensor_copy(out=cmbf[0:16, 0:fw],
                                                     in_=cma[0:16, W513:W513 + fw]),
                        reads=["cmash"], writes=["cmbf"])
                t_qw = sc.emit("g", lambda g, fw=FW: g.dma_start(out=qscr[:, 0:fw],
                                                          in_=cmbf[0:16, 0:fw]),
                               reads=["cmbf"], writes=["qscr_d"], dma=True)
                sc.emit("v", lambda v, fw=FW: v.tensor_tensor(out=cma[0:16, W513:W513 + fw],
                                                       in0=cma[0:16, 0:fw],
                                                       in1=cma[0:16, W513:W513 + fw],
                                                       op=ALU.mult),
                        reads=["cma", "cmash", "cmbf"], writes=["cmash"])
                for ri, (a, b) in enumerate(acols):
                    po = 300 if ri == 0 else 290
                    sc.emit("t", lambda t, a=a, b=b, po=po:
                            t.matmul(out=ps_sm[0:1, 0:b - a],
                                     lhsT=onesf[:], rhs=cma[0:16, W513 + a:W513 + b],
                                     start=True, stop=True),
                            reads=["cmash", "onesf"], writes=["ps_sm"])
                    sc.emit("v", lambda v, a=a, b=b:
                            v.tensor_copy(out=cma[0:1, W513 + a:W513 + b],
                                          in_=ps_sm[0:1, 0:b - a]),
                            reads=["ps_sm", "cmash"], writes=["cmash"])
                t_cbw = sc.emit("g", lambda g, fw=FW: g.dma_start(out=cbscr[:, 0:fw],
                                                           in_=cma[0:1, W513:W513 + fw]),
                               reads=["cmash"], writes=["cbscr_d"], dma=True)
                sc.dma_barrier("g", rebind=["qscr_d", "cbscr_d"],
                               scrub=["cmbf", "cmash"])
                sc.emit("g", lambda g, fw=FW:
                        g.dma_start(out=ypa[:, 0:fw],
                                    in_=cbscr[0:1, 0:fw].squeeze().partition_broadcast(P)),
                        reads=["cbscr_d"], writes=["ypa"], dma=True)
                sc.dma_barrier("g", rebind=["ypa"], scrub=["cbscr_d"])
                for n in range(NN):
                    extra = ([f"xr{i}" for i in range(NDT)]
                             + [f"xk{i}" for i in range(NDT)]) if n == 0 else []
                    sc.emit("g", lambda g, n=n, fw=FW:
                            g.dma_start(out=qxx[:, n * F: n * F + fw],
                                         in_=qscr[n:n + 1, 0:fw].squeeze()
                                         .partition_broadcast(P)),
                            reads=["qscr_d"], writes=[f"q{n}"] + extra, dma=True)
                sc.dma_barrier("g", rebind=[f"q{n}" for n in range(NN)]
                                       + [f"xr{i}" for i in range(NDT)]
                                       + [f"xk{i}" for i in range(NDT)],
                               scrub=["qscr_d"])
                for do in range(NDT):
                    for (a, b) in acols:
                        pn = f"ps_mlp{pp[0]}"
                        mmacc(lambda i, d=do: w1b[:, (i * 8 + d) * P:(i * 8 + d + 1) * P],
                              lambda i, a=a, b=b: bf1a[:, i * F + a: i * F + b],
                              NDT, pn, ps_mlp[pp[0]][:, 0:b - a], ["bf1a", "w_W1"])
                        sc.emit("s", lambda s_, d=do, a=a, b=b, p=pp[0]:
                                s_.activation(out=dff[:, d * F + a: d * F + b],
                                              in_=ps_mlp[p][:, 0:b - a],
                                              func=AF.Exp, bias=b1s[:, d:d + 1]),
                                reads=[pn, "b1_r"], writes=["dff"])
                        sc.emit("s", lambda s_, d=do, a=a, b=b:
                                s_.activation(out=dff[:, d * F + a: d * F + b],
                                              in_=dff[:, d * F + a: d * F + b],
                                              func=AF.Ln, bias=1.0),
                                reads=["dff", "cone"], writes=["dff"])
                        pp[0] ^= 1
                for do in range(NDT):
                    sc.emit("v", lambda v, d=do, fw=FW:
                            v.tensor_tensor(out=gbuf[:, 0:fw],
                                            in0=accF[:, d * F: d * F + fw],
                                            in1=dff[:, d * F: d * F + fw], op=ALU.mult),
                            reads=["accF", "dff"], writes=["gbuf0", "gbuf1"])
                    sc.emit("v", lambda v, d=do, fw=FW:
                            v.tensor_tensor(out=e2k[:, d * F: d * F + fw],
                                            in0=gbuf[:, 0:fw], in1=ypa[:, 0:fw],
                                            op=ALU.mult),
                            reads=["gbuf0", "gbuf1", "ypa"], writes=["e2k"])

                def load_diag(dt):
                    pi = dt % 2
                    sc.emit("g", lambda g, d=dt, p=pi:
                            g.dma_start(out=diagbuf[:, p * NN * P:(p + 1) * NN * P],
                                        in_=diagscr[:, d * NN * P:(d + 1) * NN * P]),
                            reads=["diagscr_d"], writes=[f"diagb{pi}"], dma=True)
                    sc.dma_barrier("g", rebind=[f"diagb{pi}"], scrub=["diagscr_d"])

                load_diag(0)
                load_diag(1)
                evac_prev = [None]
                for dt in range(NDT):
                    pacc = ps_acc if dt % 2 == 0 else ps_tr
                    pan = "ps_acc" if dt % 2 == 0 else "ps_tr"
                    for ri, (a, b) in enumerate(acols):
                        pcol = a if ri == 0 else 512
                        sc.emit("t", lambda t, d=dt, a=a, b=b, pc=pcol, pa=pacc:
                                t.matmul(out=pa[:, pc:pc + b - a], lhsT=identb[:],
                                         rhs=e2k[:, d * F + a: d * F + b],
                                         start=True, stop=False),
                                reads=["e2k", "identb_r"],
                                writes=[pan] if ri == 0 else [])
                    def emit_g(n, dt=dt, FW=FW):
                        pi = n % 2
                        gsl = gbuf[:, pi * W513: pi * W513 + FW]
                        ac = dt * NN + n
                        sc.emit("s", lambda s_, d=dt, a=ac, g=gsl, fw=FW:
                                s_.activation(out=g, in_=dff[:, d * F: d * F + fw],
                                              func=AF.Exp, scale=A_sb[:, a:a + 1]),
                                reads=["dff", "A_r"], writes=[f"gbuf{pi}"])

                    def emit_ypwp(n, dt=dt, FW=FW):
                        pi = n % 2
                        p2 = (n // 2) % 2
                        gsl = gbuf[:, pi * W513: pi * W513 + FW]
                        wb = wpa if p2 == 0 else cma
                        wsl = wb[:, pi * W513: pi * W513 + FW]
                        wnm = [f"wp{pi}"] if p2 == 0 else ["cma", "cmash"]
                        ac = dt * NN + n
                        sc.emit("v", lambda v, g=gsl, fw=FW:
                                v.tensor_scalar(out=ypa[:, 0:fw], in0=g, scalar1=1.0,
                                                scalar2=0.0, op0=ALU.subtract,
                                                op1=ALU.min),
                                reads=[f"gbuf{pi}"], writes=["ypa"])
                        sc.emit("v", lambda v, d=dt, a=ac, w=wsl, fw=FW:
                                v.scalar_tensor_tensor(out=w, in0=dff[:, d * F: d * F + fw],
                                                       scalar=A_sb[:, a:a + 1],
                                                       in1=ypa[:, 0:fw],
                                                       op0=ALU.mult, op1=ALU.max),
                                reads=["dff", "A_r", "ypa"], writes=wnm)

                    def emit_v_pair(n, FW=FW):
                        p2 = (n // 2) % 2
                        wb = wpa if p2 == 0 else cma
                        w2 = wb[:, 0: W513 + FW]
                        v2 = vza[:, 2 * p2 * W513: 2 * p2 * W513 + W513 + FW]
                        rnm = ["wp0", "wp1"] if p2 == 0 else ["cma", "cmash"]
                        wnm = (["vzaA", "vzaB"], ["vzaC", "vzaD"])[p2]
                        sc.emit("s", lambda s_, w=w2, vv=v2:
                                s_.activation(out=vv, in_=w, func=AF.Exp, scale=-1.0),
                                reads=rnm, writes=wnm)

                    def emit_z(n, FW=FW):
                        sl = 2 * ((n // 2) % 2) + (n % 2)
                        nm = ("vzaA", "vzaB", "vzaC", "vzaD")[sl]
                        vsl = vza[:, sl * W513: sl * W513 + FW]
                        sc.emit("v", lambda v, n=n, vv=vsl, fw=FW:
                                v.tensor_tensor(out=vv, in0=vv,
                                                in1=qxx[:, n * F: n * F + fw],
                                                op=ALU.mult),
                                reads=[nm, f"q{n}"], writes=[nm])

                    def emit_mm(n, dt=dt, FW=FW, acols=acols, pacc=pacc, pan=pan):
                        sl = 2 * ((n // 2) % 2) + (n % 2)
                        zn = ("vzaA", "vzaB", "vzaC", "vzaD")[sl]
                        zsl = vza[:, sl * W513: sl * W513 + FW]
                        pi_d = dt % 2
                        for ri, (a, b) in enumerate(acols):
                            pcol = a if ri == 0 else 512
                            sc.emit("t", lambda t, a=a, b=b, pc=pcol, n=n, z=zsl, p=pi_d, pa=pacc:
                                    t.matmul(out=pa[:, pc:pc + b - a],
                                             lhsT=diagbuf[:, (p * NN + n) * P:(p * NN + n + 1) * P],
                                             rhs=z[:, a:b],
                                             start=False, stop=(n == NN - 1)),
                                    reads=[zn] + ([f"diagb{pi_d}"] if n in (0, NN - 1) else []),
                                    writes=[pan] if (n == NN - 1 and ri == len(acols) - 1) else [])

                    emit_g(0)
                    emit_g(1)
                    for n0 in range(0, NN, 2):
                        emit_ypwp(n0)
                        if n0 + 2 < NN:
                            emit_g(n0 + 2)
                        emit_ypwp(n0 + 1)
                        if n0 + 3 < NN:
                            emit_g(n0 + 3)
                        emit_v_pair(n0)
                        if n0 == 0 and evac_prev[0] is not None:
                            evac_prev[0]()
                            evac_prev[0] = None
                        if n0 >= 2:
                            emit_z(n0 - 2)
                            emit_mm(n0 - 2)
                            emit_z(n0 - 1)
                            emit_mm(n0 - 1)
                    emit_z(NN - 2)
                    emit_mm(NN - 2)
                    emit_z(NN - 1)
                    emit_mm(NN - 1)
                    if dt + 2 < NDT:
                        load_diag(dt + 2)
                    def do_evac(dt=dt, FW=FW, pacc=pacc, pan=pan):
                        sc.emit("v", lambda v, d=dt, w=min(FW, 512), pa=pacc:
                                v.tensor_copy(out=accF[:, d * F: d * F + w],
                                              in_=pa[:, 0:w]),
                                reads=[pan], writes=["accF"])
                        if FW == 513:
                            sc.emit("v", lambda v, d=dt, pa=pacc:
                                    v.tensor_copy(out=accF[:, d * F + 512: d * F + 513],
                                                  in_=pa[:, 512:513]),
                                    reads=[pan], writes=["accF"])
                    if dt == NDT - 1:
                        do_evac()
                    else:
                        evac_prev[0] = do_evac

                for lt in range(4):
                    col = TKO + lt * P
                    ptr = ps_tr if lt % 2 == 0 else ps_acc
                    ptn = "ps_tr" if lt % 2 == 0 else "ps_acc"
                    for dt in range(NDT):
                        sc.emit("t", lambda t, d=dt, c=col, pa=ptr:
                                t.transpose(out=pa[:, d * P:(d + 1) * P],
                                            in_=accF[:, d * F + c: d * F + c + P],
                                            identity=identf[:]),
                                reads=["accF", "identf_r"], writes=[ptn])
                    sc.emit("v", lambda v, l=lt, pa=ptr:
                            v.scalar_tensor_tensor(out=xstage[:, l * D:(l + 1) * D],
                                                   in0=pa[:, 0:D], scalar=1.0,
                                                   in1=xstage[:, l * D:(l + 1) * D],
                                                   op0=ALU.mult, op1=ALU.add),
                            reads=[ptn, f"xst{lt}"], writes=[f"xst{lt}"])
                emit_stats_group([xstage[:, lt * D:(lt + 1) * D] for lt in range(4)],
                                 [[f"xst{lt}"] for lt in range(4)], gbuf[:, 0:1024])
                for lt in range(4):
                    emit_ln_apply(xstage[:, lt * D:(lt + 1) * D], lt,
                                  lnstage[:, 0:1024], [f"xst{lt}"], ["lnstage", "wp0"])
                    for dt in range(NDT):
                        sc.emit("t", lambda t, d=dt:
                                t.transpose(out=ps_bf[:, d * P:(d + 1) * P],
                                            in_=lnstage[:, d * P:(d + 1) * P],
                                            identity=identb[:]),
                                reads=["lnstage", "wp0", "identb_r"], writes=["ps_bf"])
                    lcol = 1 + lt * P
                    sc.emit("v", lambda v, c=lcol:
                            [v.tensor_copy(out=bf2a[:, d * F + c: d * F + c + P],
                                           in_=ps_bf[:, d * P:(d + 1) * P])
                             for d in range(NDT)][-1],
                            reads=["ps_bf"], writes=["bf2a"])
                if c0:
                    for dt in range(NDT):
                        sc.emit("t", lambda t, d=dt:
                                t.transpose(out=ps_tr[0:1, d * P:(d + 1) * P],
                                            in_=accF[:, d * F: d * F + 1],
                                            identity=identf[:]),
                                reads=["accF", "identf_r"], writes=["ps_tr"])
                    sc.emit("v", lambda v:
                            v.scalar_tensor_tensor(out=xhalo[:], in0=ps_tr[0:1, 0:D],
                                                   scalar=1.0, in1=xhalo[:],
                                                   op0=ALU.mult, op1=ALU.add),
                            reads=["ps_tr", "xhalo"], writes=["xhalo"])
                    emit_ln(xhalo[:], 1, lnhalo[0:1, 0:1024], [], gbuf[0:1, 0:1024],
                            hstats, "hstats", ["xhalo"], ["gbuf0", "gbuf1"], ["lnhalo", "wp1"])
                    sc.emit("v", lambda v:
                            v.tensor_scalar(out=lnhalo[0:1, 0:1024],
                                            in0=lnhalo[0:1, 0:1024],
                                            scalar1=maskp[0:1, 0:1], scalar2=None,
                                            op0=ALU.mult),
                            reads=["lnhalo", "maskv_r"], writes=["lnhalo", "wp1"])
                    for dt in range(NDT):
                        sc.emit("t", lambda t, d=dt:
                                t.transpose(out=ps_bf[0:P, 960 + 2 * d:961 + 2 * d],
                                            in_=lnhalo[0:1, d * P:(d + 1) * P],
                                            identity=identb[0:1, 0:1]),
                                reads=["lnhalo", "wp1", "identb_r"], writes=["ps_bf"])
                    sc.emit("v", lambda v:
                            [v.tensor_copy(out=bf2a[:, d * F: d * F + 1],
                                           in_=ps_bf[:, 960 + 2 * d:961 + 2 * d])
                             for d in range(NDT)][-1],
                            reads=["ps_bf"], writes=["bf2a"])
                else:
                    sc.emit("v", lambda v:
                            [v.tensor_copy(out=bf2a[:, d * F: d * F + 1],
                                           in_=carrycol[:, d:d + 1])
                             for d in range(NDT)][-1],
                            reads=["carry"], writes=["bf2a"])
                sc.emit("v", lambda v:
                        [v.tensor_copy(out=carrycol[:, d:d + 1],
                                       in_=bf2a[:, d * F + 512: d * F + 513])
                         for d in range(NDT)][-1],
                        reads=["bf2a"], writes=["carry"])

                for dt in range(NDT):
                    dc = dt * F
                    sc.emit("v", lambda v, d=dc:
                            v.tensor_tensor(out=gbuf[:, 0:512],
                                            in0=bf2a[:, d + 1: d + 513],
                                            in1=bf2a[:, d: d + 512], op=ALU.subtract),
                            reads=["bf2a"], writes=["gbuf0"])
                    sc.emit("v", lambda v, d=dt, dc=dc:
                            v.scalar_tensor_tensor(out=qxx[:, d * 512:(d + 1) * 512],
                                                   in0=gbuf[:, 0:512],
                                                   scalar=murs[:, d:d + 1],
                                                   in1=bf2a[:, dc: dc + 512],
                                                   op0=ALU.mult, op1=ALU.add),
                            reads=["gbuf0", "mur_r", "bf2a"], writes=["xr" + str(dt)])
                    sc.emit("v", lambda v, d=dt, dc=dc:
                            v.scalar_tensor_tensor(out=qxx[:, (8 + d) * 512:(9 + d) * 512],
                                                   in0=gbuf[:, 0:512],
                                                   scalar=muks[:, d:d + 1],
                                                   in1=bf2a[:, dc: dc + 512],
                                                   op0=ALU.mult, op1=ALU.add),
                            reads=["gbuf0", "muk_r", "bf2a"], writes=["xk" + str(dt)])
                for do in range(NDT):
                    pn = f"ps_mlp{pp[0]}"
                    mmacc(lambda i, d=do: wrtb[:, (i * 8 + d) * P:(i * 8 + d + 1) * P],
                          lambda i: qxx[:, i * 512:(i + 1) * 512],
                          NDT, pn, ps_mlp[pp[0]][:, 0:512],
                          [f"xr{i}" for i in range(NDT)] + ["w_WrT"])
                    sc.emit("s", lambda s_, d=do, p=pp[0]:
                            s_.activation(out=accF[:, d * F: d * F + 512],
                                          in_=ps_mlp[p][:, 0:512],
                                          func=AF.Tanh, scale=0.5),
                            reads=[pn], writes=["accF"])
                    sc.emit("v", lambda v, d=do:
                            v.tensor_scalar(out=accF[:, d * F: d * F + 512],
                                            in0=accF[:, d * F: d * F + 512],
                                            scalar1=0.5, scalar2=0.5,
                                            op0=ALU.mult, op1=ALU.add),
                            reads=["accF"], writes=["accF"])
                    pp[0] ^= 1
                for do in range(NDT):
                    pn = f"ps_mlp{pp[0]}"
                    mmacc(lambda i, d=do: wktb[:, (i * 8 + d) * P:(i * 8 + d + 1) * P],
                          lambda i: qxx[:, (8 + i) * 512:(9 + i) * 512],
                          NDT, pn, ps_mlp[pp[0]][:, 0:512],
                          [f"xk{i}" for i in range(NDT)] + ["w_WkT"])
                    sc.emit("s", lambda s_, p=pp[0]:
                            s_.activation(out=gbuf[:, 0:512], in_=ps_mlp[p][:, 0:512],
                                          func=AF.Identity),
                            reads=[pn], writes=["gbuf0"])
                    sc.emit("v", lambda v, d=do:
                            v.scalar_tensor_tensor(out=e2k[:, d * F: d * F + 512],
                                                   in0=gbuf[:, 0:512], scalar=0.0,
                                                   in1=gbuf[:, 0:512],
                                                   op0=ALU.max, op1=ALU.mult),
                            reads=["gbuf0"], writes=["e2k"])
                    pp[0] ^= 1
                for do in range(NDT):
                    pn = f"ps_mlp{pp[0]}"
                    mmacc(lambda i, d=do: wvtb[:, (i * 8 + d) * P:(i * 8 + d + 1) * P],
                          lambda i: e2k[:, i * F: i * F + 512],
                          NDT, pn, ps_mlp[pp[0]][:, 0:512], ["e2k", "w_WvT"])
                    sc.emit("v", lambda v, d=do, p=pp[0]:
                            v.tensor_tensor(out=dff[:, d * F: d * F + 512],
                                            in0=ps_mlp[p][:, 0:512],
                                            in1=accF[:, d * F: d * F + 512],
                                            op=ALU.mult),
                            reads=[pn, "accF"], writes=["dff"])
                    pp[0] ^= 1
                y2s = qxx.bitcast(F32)
                for lt in range(4):
                    ptr = ps_tr if lt % 2 == 0 else ps_acc
                    ptn = "ps_tr" if lt % 2 == 0 else "ps_acc"
                    for dt in range(NDT):
                        sc.emit("t", lambda t, d=dt, l=lt, pa=ptr:
                                t.transpose(out=pa[:, d * P:(d + 1) * P],
                                            in_=dff[:, d * F + l * P: d * F + l * P + P],
                                            identity=identf[:]),
                                reads=["dff", "identf_r"], writes=[ptn])
                    sc.emit("v", lambda v, l=lt, y2=y2s, pa=ptr:
                            v.scalar_tensor_tensor(out=y2[:, l * D:(l + 1) * D],
                                                   in0=pa[:, 0:D], scalar=1.0,
                                                   in1=xstage[:, l * D:(l + 1) * D],
                                                   op0=ALU.mult, op1=ALU.add),
                            reads=[ptn, f"xst{lt}"]
                                  + ([f"xr{i}" for i in range(4 * lt, 4 * lt + 4)]
                                     if lt < 2 else
                                     [f"xk{i}" for i in range(4 * (lt - 2), 4 * (lt - 2) + 4)])
                                  + [f"q{n}" for n in range(max(0, 4 * lt - 1),
                                                            min(NN, 4 * lt + 4))],
                            writes=[f"y2s{lt}"])
                    sc.emit("g", lambda g, l=lt, r=row0, y2=y2s:
                            g.dma_start(out=ys[r - 1 + l * P: r - 1 + (l + 1) * P, :],
                                        in_=y2[:, l * D:(l + 1) * D]),
                            reads=[f"y2s{lt}"], writes=[f"ysout{ci}_{lt}"], dma=True)

                obar = sc.dma_barrier(
                    "g",
                    scrub=[f"y2s{l_}" for l_ in range(4)]
                          + [f"xr{i_}" for i_ in range(NDT)]
                          + [f"xk{i_}" for i_ in range(NDT)]
                          + [f"q{n_}" for n_ in range(NN)])

            fin = sc.tok("gd")
            sc.emit("g", lambda g: g.engine_nop(), waits=[fin])

            sc.check_dma_waits()

            @block.vector
            def _(v):
                sc.replay("v", v, sems)

            @block.scalar
            def _(s_):
                sc.replay("s", s_, sems)

            @block.tensor
            def _(t):
                sc.replay("t", t, sems)

            @block.gpsimd
            def _(g):
                sc.replay("g", g, sems)

            @block.sync
            def _(q):
                sc.replay("q", q, sems)

    return nc


_NC_CACHE = []


def kernel(**inputs):
    x = np.ascontiguousarray(inputs["L0_attn_x"], dtype=np.float32)
    B, L, _ = x.shape
    per = L // 4

    def f32(a):
        return np.ascontiguousarray(a, dtype=np.float32)

    weights = {
        "Wl1": f32(inputs["L0_attn_Wl1"]),
        "Wl2": f32(inputs["L0_attn_Wl2"]),
        "Wf1": f32(inputs["L0_attn_Wf1"]),
        "Wf2": f32(inputs["L0_attn_Wf2"]),
        "W1": f32(inputs["L0_attn_W1"]),
        "W23": np.ascontiguousarray(
            np.concatenate([f32(inputs["L0_attn_W2"]), f32(inputs["L0_attn_W3"])],
                           axis=1)),
        "WrT": np.ascontiguousarray(f32(inputs["L0_ffn_W_r"]).T),
        "WkT": np.ascontiguousarray(f32(inputs["L0_ffn_W_k"]).T),
        "WvT": np.ascontiguousarray(f32(inputs["L0_ffn_W_v"]).T),
        "A": f32(inputs["L0_attn_A"]),
        "S": f32(inputs["L0_attn_state_init"]).reshape(D, NN),
        "bl1": f32(inputs["L0_attn_bl1"]).reshape(P, 1),
        "bl2": f32(inputs["L0_attn_bl2"]).reshape(D, 1),
        "bf1": f32(inputs["L0_attn_bf1"]).reshape(P, 1),
        "bf2": f32(inputs["L0_attn_bf2"]).reshape(D, 1),
        "b1": f32(inputs["L0_attn_b1"]).reshape(D, 1),
        "b23": np.ascontiguousarray(
            np.concatenate([f32(inputs["L0_attn_b2"]),
                            f32(inputs["L0_attn_b3"])]).reshape(32, 1)),
        "mur": f32(inputs["L0_ffn_mu_r"]).reshape(D, 1),
        "muk": f32(inputs["L0_ffn_mu_k"]).reshape(D, 1),
        "identf": np.eye(P, dtype=np.float32),
        "identb": np.eye(P).astype(BFDT),
    }

    if not _NC_CACHE:
        _NC_CACHE.append(build_program())
    nc = _NC_CACHE[0]

    in_maps = []
    for c in range(8):
        b, s = divmod(c, 4)
        t0 = s * per
        shard = np.zeros((ROWS, D), dtype=np.float32)
        shard[1:] = x[b, t0:t0 + per]
        if t0 > 0:
            shard[0] = x[b, t0 - 1]
        im = {"xs": shard,
              "maskv": np.full((P, 1), 0.0 if t0 == 0 else 1.0, dtype=np.float32)}
        im.update(weights)
        in_maps.append(im)

    res = run_bass_kernel_spmd(nc, in_maps, core_ids=list(range(8)))
    out = np.empty((B, L, D), dtype=np.float32)
    for c in range(8):
        b, s = divmod(c, 4)
        t0 = s * per
        out[b, t0:t0 + per] = res.results[c]["ys"]
    return out
